# version 30
# baseline (speedup 1.0000x reference)
import numpy as np
import ml_dtypes
import concourse.bass as bass
import concourse.bass_utils as _bu

if not getattr(_bu, "_ldwopt_patched", False):
    _orig_run_command = _bu.run_command

    def _run_command_ldwopt(argv, **kwargs):
        argv = ["--enable-ldw-opt=true" if a == "--enable-ldw-opt=false" else a
                for a in argv]
        return _orig_run_command(argv, **kwargs)

    _bu.run_command = _run_command_ldwopt
    _bu._ldwopt_patched = True
import concourse.mybir as mybir
from concourse.bass_utils import run_bass_kernel_spmd
from contextlib import ExitStack

F32 = mybir.dt.float32
BF16 = mybir.dt.bfloat16
AF = mybir.ActivationFunctionType
ALU = mybir.AluOpType
AXX = mybir.AxisListType.X

T = 2048
NT = 16
HC = 8
H = 1024
E = 8
NH = 16
QGS = 1024
NQG = 2
SI = 1024

BFNP = np.dtype(ml_dtypes.bfloat16)


def build_program(dbg=False):
    nc = bass.Bass("TRN2", target_bir_lowering=False)

    x_d = nc.dram_tensor("x", [T, H], F32, kind="ExternalInput")
    wq_d = nc.dram_tensor("wq", [H, H], F32, kind="ExternalInput")
    wout_d = nc.dram_tensor("wout", [H, H], F32, kind="ExternalInput")
    wkvd_d = nc.dram_tensor("wkvd", [H, 128], F32, kind="ExternalInput")
    wkvu_d = nc.dram_tensor("wkvu", [128, 2 * H], F32, kind="ExternalInput")
    gwT_d = nc.dram_tensor("gwT", [H, E], F32, kind="ExternalInput")
    lb_d = nc.dram_tensor("lb", [1, E], F32, kind="ExternalInput")
    egwT_d = nc.dram_tensor("egwT", [E, H, 512], F32, kind="ExternalInput")
    eupT_d = nc.dram_tensor("eupT", [E, H, 512], F32, kind="ExternalInput")
    edwT_d = nc.dram_tensor("edwT", [E, 512, H], F32, kind="ExternalInput")
    sgwT_d = nc.dram_tensor("sgwT", [H, SI], F32, kind="ExternalInput")
    supT_d = nc.dram_tensor("supT", [H, SI], F32, kind="ExternalInput")
    sdwT_d = nc.dram_tensor("sdwT", [SI, H], F32, kind="ExternalInput")
    ident_d = nc.dram_tensor("ident", [128, 128], BF16, kind="ExternalInput")
    tri_d = nc.dram_tensor("tri", [128, 128], BF16, kind="ExternalInput")
    out_d = nc.dram_tensor("out", [T, H], F32, kind="ExternalOutput")
    attn_d = nc.dram_tensor("attn_sc", [T, H], F32,
                            kind="ExternalOutput" if dbg else "Internal")
    cci_d = nc.dram_tensor("cci", [1, 16], F32)
    cco_d = nc.dram_tensor("cco", [1, 16], F32)
    if dbg:
        dxstg_d = nc.dram_tensor("d_xstg", [128, 2048], F32, kind="ExternalOutput")
        dsq_d = nc.dram_tensor("d_sq", [128, 1024], F32, kind="ExternalOutput")
        dxnt_d = nc.dram_tensor("d_xnt", [128, 2048], BF16, kind="ExternalOutput")
        dscrs_d = nc.dram_tensor("d_scrs", [128, 8], F32, kind="ExternalOutput")
        dsel_d = nc.dram_tensor("d_sel", [128, 128], F32, kind="ExternalOutput")
        dstats_d = nc.dram_tensor("d_stats", [1, 16], F32, kind="ExternalOutput")
        dexpm_d = nc.dram_tensor("d_expm", [128, 8], F32, kind="ExternalOutput")
        dt8_d = nc.dram_tensor("d_t8", [128, 8], F32, kind="ExternalOutput")
        dlg_d = nc.dram_tensor("d_lg", [128, 8], F32, kind="ExternalOutput")
        dc0_d = nc.dram_tensor("d_c0", [128, 4], F32, kind="ExternalOutput")
        dy_d = nc.dram_tensor("d_y", [128, 8192], BF16, kind="ExternalOutput")
        daccf_d = nc.dram_tensor("d_accf", [128, NT * H], F32, kind="ExternalOutput")
        dxd_d = nc.dram_tensor("d_xd", [128, 1024], F32, kind="ExternalOutput")
        dstg_d = nc.dram_tensor("d_stg", [128, 2048], F32, kind="ExternalOutput")
        dxstg2_d = nc.dram_tensor("d_xstg2", [128, 2048], F32, kind="ExternalOutput")
        dxn_d = nc.dram_tensor("d_xn", [128, HC * T], BF16, kind="ExternalOutput")
        dgst_d = nc.dram_tensor("d_gst", [1, 16], F32, kind="ExternalOutput")
        dnegb_d = nc.dram_tensor("d_negb", [128, 1], F32, kind="ExternalOutput")
        dlat_d = nc.dram_tensor("d_lat", [128, T], BF16, kind="ExternalOutput")
        dq_d = nc.dram_tensor("d_q", [128, HC * T], BF16, kind="ExternalOutput")
        dk_d = nc.dram_tensor("d_k", [128, HC * T], BF16, kind="ExternalOutput")
        dacc_d = nc.dram_tensor("d_acc", [128, NT * H], F32, kind="ExternalOutput")

    es = ExitStack()

    def sb(name, shape, dt):
        return es.enter_context(nc.sbuf_tensor(name, shape, dt))

    IDENT = sb("IDENT", [128, 128], BF16)
    TRI = sb("TRI", [128, 128], BF16)
    ONES_R = sb("ONES_R", [1, 128], BF16)
    ONES_C = sb("ONES_C", [128, 1], BF16)
    GW = sb("GW", [128, 64], BF16)
    LBB = sb("LBB", [1, 8], BF16)
    XN = sb("XN", [128, HC * T], BF16)
    SEL = sb("SEL", [128, 128], F32)
    SELW = sb("SELW", [128, 16], BF16)
    SCRS = sb("SCRS", [128, 8], F32)
    EXPM = sb("EXPM", [128, 8], F32)
    T8 = sb("T8", [128, 8], F32)
    AUX8 = sb("AUX8", [128, 8], F32)
    C1 = sb("C1", [128, 1], F32)
    C2 = sb("C2", [128, 1], F32)
    C3 = sb("C3", [128, 1], F32)
    C4 = sb("C4", [128, 1], F32)
    C5 = sb("C5", [128, 1], F32)
    C6 = sb("C6", [128, 1], F32)
    RW = sb("RW", [128, 10], F32)
    GLN = sb("GLN", [1, 8], F32)
    DC0 = sb("DC0", [128, 4], F32)
    DJV = sb("DJV", [128, 8], F32)
    DJS = sb("DJS", [128, 8], F32)
    WF = sb("WF", [128, 128], F32)
    SELB = sb("SELB", [128, 128], BF16)
    WFB = sb("WFB", [128, 128], BF16)
    STATS = sb("STATS", [1, 16], F32)
    GSTATS = sb("GSTATS", [1, 16], F32)
    WMB = sb("WMB", [1, 8], BF16)
    WMEANB = sb("WMEANB", [128, 8], F32)
    NEGB = sb("NEGB", [128, 1], F32)
    QKM = sb("QKM", [128, 66], F32)
    QK2 = sb("QK2", [128, 2], BF16)
    MX2 = sb("MX2", [1, 2], F32)
    MR2 = sb("MR2", [1, 2], F32)
    MR2B = sb("MR2B", [1, 1], BF16)
    XSTG = sb("XSTG", [128, 2 * H], F32)
    W32 = sb("W32", [128, 4096], F32)
    AOUT = sb("AOUT", [128, 2 * H], F32)
    SQ = AOUT[:, 0:1024]
    XNT = AOUT[:, 1024:2048].bitcast(BF16)
    SROW32 = XSTG[0:1, 0:1024]
    SROW = XSTG[0:1, 1024:1536].bitcast(BF16)
    GW32 = W32[:, 3072:3136]
    LBB32 = W32[0:1, 3136:3144]
    YODD = sb("YODD", [64, 1024], BF16)
    WS = sb("WS", [128, 1], F32)

    AR = sb("AR", [128, 70144], BF16)

    def bfv(off_b, nb):
        return AR[:, off_b // 2: (off_b + nb) // 2]

    def r3(ap):
        return ap.rearrange("(c p) n -> p c n", p=128)

    def d3(ap, c_):
        return ap.rearrange("p (c n) -> p c n", c=c_)

    OFF_R1, OFF_PT, OFF_Q, OFF_K, OFF_V, OFF_Y = (
        0, 20 * 1024, 24 * 1024, 56 * 1024, 88 * 1024, 121 * 1024)

    WKVD = bfv(OFF_R1, 2048)
    WKVU = bfv(OFF_R1 + 2048, 4096)
    LATB = bfv(OFF_R1 + 2048 + 4096, 4096)
    WQB = bfv(OFF_R1 + 2048, 16384)
    WOUTB = WQB
    PT = [bfv(OFF_PT, 2048), bfv(OFF_PT + 2048, 2048)]
    QB = bfv(OFF_Q, 32768)
    KB = bfv(OFF_K, 32768)
    VB = bfv(OFF_V, 33280)
    YQ = bfv(OFF_Y, 16384)
    ACC = AR[:, OFF_Q // 2: (OFF_Q + 65536) // 2].bitcast(F32)
    EW = {0: (bfv(OFF_V, 8192), bfv(OFF_V + 8192, 8192), bfv(OFF_V + 16384, 8192)),
          1: (bfv(OFF_Y, 8192), bfv(OFF_Y + 8192, 8192), bfv(OFF_R1 + 12288, 8192))}
    H_CH = [bfv(OFF_R1, 4096), bfv(OFF_R1 + 4096, 4096),
            bfv(OFF_R1 + 8192, 4096), bfv(OFF_PT, 4096)]
    SGE = bfv(OFF_V + 24576, 4096)
    SD = bfv(OFF_V, 16384)
    HS_CH = [bfv(OFF_V + 16384 + 4096 * i, 4096) for i in range(4)] + \
            [bfv(OFF_Y + 4096 * i, 4096) for i in range(4)]
    SGC = [bfv(OFF_R1, 2048), bfv(OFF_R1 + 2048, 2048)]
    SUC = [bfv(OFF_R1 + 4096, 2048), bfv(OFF_R1 + 6144, 2048)]
    SGS = bfv(OFF_R1 + 8192, 4096)

    PSD0 = es.enter_context(nc.psum_tensor("PSD0", [128, 1024], F32))
    PSD1 = es.enter_context(nc.psum_tensor("PSD1", [128, 1024], F32))
    PSD2 = es.enter_context(nc.psum_tensor("PSD2", [128, 1024], F32))
    PSD4 = es.enter_context(nc.psum_tensor("PSD4", [128, 1024], F32))
    PSB = PSD2[:, 0:512].bitcast(BF16)

    sem_names = ["sxA", "sxB", "swA", "swB", "swC", "swD", "gwA", "gwB",
                 "gwC", "gwD", "gxA", "gxB", "gyo", "sone", "gout", "gcc",
                 "gacc", "sout", "tn", "vc", "sc", "vcast", "gc"]
    sems = {n: es.enter_context(nc.semaphore(n)) for n in sem_names}
    sched = {"sync": [], "gpsimd": [], "tensor": [], "vector": [], "scalar": []}
    cnt = {n: 0 for n in sem_names}

    def do(eng, fn, inc=None, amt=1, waits=()):
        w = [(s, v) for (s, v) in waits if v > 0]
        sched[eng].append((w, fn, inc, amt))
        if inc is not None:
            cnt[inc] += amt
            return cnt[inc]
        return None

    def c(s):
        return cnt[s]

    def barrier(snap=None):
        if snap is None:
            snap = [(s, v) for s, v in cnt.items() if v > 0]
        for eng in sched:
            sched[eng].append((snap, None, None, 0))

    def MM(out, lhsT, rhs, start, stop, waits=(), inc=True):
        return do("tensor",
                  lambda e, o=out, l=lhsT, r=rhs, st=start, sp=stop:
                  e.matmul(o, l, r, start=st, stop=sp),
                  inc=("tn" if inc else None), waits=waits)

    def TR(out, in_, waits=()):
        return do("tensor", lambda e, o=out, i=in_: e.transpose(o, i, IDENT[:, :]),
                  inc="tn", waits=waits)

    def DMA(q, out, in_, inc, waits=(), accum=None):
        if accum is None:
            return do(q, lambda e, o=out, i=in_: e.dma_start(out=o, in_=i),
                      inc=inc, amt=16, waits=waits)
        return do(q, lambda e, o=out, i=in_, a=accum:
                  e.dma_start(out=o, in_=i, accum_op=a), inc=inc, amt=16, waits=waits)

    def VOP(fn, waits=(), inc="vc"):
        return do("vector", fn, inc=inc, waits=waits)

    def SOP(fn, waits=(), inc="sc"):
        return do("scalar", fn, inc=inc, waits=waits)

    wpar = [0]
    w32_free = [0, 0, 0, 0]
    nslot = [2]
    stg_gate = {}
    stg_sem = [("swA", "gwA"), ("swB", "gwB"), ("swC", "gwC"), ("swD", "gwD")]
    STG = [W32[:, 0:2048], W32[:, 2048:4096], AOUT, XSTG]

    def VOPW(fn, waits=(), inc="vc"):
        return do("vector", fn, inc=inc, waits=list(waits) + [("vc", c("vc"))])

    def VPUB():
        return VOP(lambda e: e.memset(DJV[:, :], 0.0),
                   waits=[("vc", c("vc"))])

    def SPUB():
        return SOP(lambda e: e.memzero(DJS[:, :]))

    def stream_half(src_ap, dst_ap, extra=(), ncols=2048, ndim3=None):
        p = wpar[0] % nslot[0]
        wpar[0] += 1
        swn, gwn = stg_sem[p]
        w32dst = STG[p][:, 0:ncols]
        wts = [("vcast", w32_free[p])] + list(extra) + stg_gate.pop(p, [])
        if ndim3 is not None and ndim3 % 2 == 0:
            w3 = d3(w32dst, ndim3)
            DMA("sync", w3[:, 0: ndim3 // 2, :], src_ap[:, 0: ndim3 // 2, :],
                swn, waits=wts)
            DMA("gpsimd", w3[:, ndim3 // 2: ndim3, :],
                src_ap[:, ndim3 // 2: ndim3, :], gwn, waits=wts)
        elif ndim3 is not None:
            DMA("sync", d3(w32dst, ndim3), src_ap, swn, waits=wts)
        else:
            DMA("sync", w32dst[:, 0: ncols // 2], src_ap[:, 0: ncols // 2],
                swn, waits=wts)
            DMA("gpsimd", w32dst[:, ncols // 2: ncols],
                src_ap[:, ncols // 2: ncols], gwn, waits=wts)
        w32_free[p] = do("vector",
                         lambda e, p=p, d=dst_ap, n=ncols: e.tensor_copy(
                             d, STG[p][:, 0:n]),
                         inc="vcast", waits=[(swn, c(swn)), (gwn, c(gwn))])
        return c("vcast")

    DMA("sync", IDENT[:, :], ident_d[:, :], "sone")
    DMA("sync", TRI[:, :], tri_d[:, :], "sone")
    DMA("sync", d3(GW32[:, :], 8), r3(gwT_d[:, :]), "sone")
    DMA("sync", LBB32[:, :], lb_d[:, :], "sone")
    DMA("sync", d3(W32[:, 0:1024], 8), r3(wkvd_d[:, :]), "sone")
    DMA("sync", W32[:, 1024:3072], wkvu_d[:, :], "sone")
    w_sone = c("sone")
    VOP(lambda e: e.memset(ONES_R[:, :], 1.0), waits=[("sone", w_sone)])
    VOP(lambda e: e.memset(ONES_C[:, :], 1.0))
    VOP(lambda e: e.tensor_copy(GW[:, :], GW32[:, :]))
    VOP(lambda e: e.tensor_copy(LBB[:, :], LBB32[:, :]))
    VOP(lambda e: e.tensor_copy(WKVD[:, :], W32[:, 0:1024]))
    VOP(lambda e: e.tensor_copy(WKVU[:, :], W32[:, 1024:3072]))
    v_consts = c("vc")
    VOP(lambda e: e.memset(QKM[:, :], 0.0))
    VOP(lambda e: e.memset(
        VB[:, :].rearrange("p (k h) -> p k h", h=65)[:, :, 64:65], 1.0))
    v_vones = c("vc")

    PSBP = [PSB, PSD4[:, 0:512].bitcast(BF16)]
    x_free = {0: 0, 1: 0}
    sq_free = {0: 0, 1: 0}
    xnt_free = {0: 0, 1: 0}
    psb_evacp = {0: 0, 1: 0}
    lg_free_p = {0: 0, 1: 0}
    expm_read = 0
    s_expm = 0
    s_sq = 0
    s_rstd = 0
    v_xnt_prev = 0
    v_c4_prev = 0
    s_latcopy = 0
    pend_lat = None
    xdma = {}

    def emit_xdma(t):
        par = t % 2
        sxn = "sxA" if par == 0 else "sxB"
        xt = XSTG[:, 1024 * par: 1024 * par + 1024]
        wts = [("vc", x_free[par]), ("sc", sq_free[par])]
        DMA("sync", xt[:, 0:512], x_d[128 * t: 128 * t + 128, 0:512], sxn,
            waits=wts)
        DMA("gpsimd", xt[:, 512:1024], x_d[128 * t: 128 * t + 128, 512:1024],
            "gxA" if par == 0 else "gxB", waits=wts)
        xdma[t] = [(sxn, c(sxn)),
                   ("gxA" if par == 0 else "gxB",
                    c("gxA" if par == 0 else "gxB"))]

    def emit_sums(t):
        nonlocal s_sq
        par = t % 2
        xt = XSTG[:, 1024 * par: 1024 * par + 1024]
        SOP(lambda e, x=xt: e.activation(SQ, x, AF.Copy, accum_out=C1[:, :]),
            waits=xdma[t] + [("vc", v_c4_prev)])
        sq_free[par] = SOP(lambda e, x=xt: e.activation(SQ, x, AF.Square,
                                                        accum_out=C2[:, :]))
        s_sq = SPUB()

    def emit_latent_mm(s):
        pd = PSD1 if s < 2 else PSD2
        sl = pd[:, 512 * (s % 2): 512 * (s % 2) + 512]
        for ci in range(HC):
            MM(sl, WKVD[:, 128 * ci: 128 * ci + 128],
               XN[:, T * ci + 512 * s: T * ci + 512 * s + 512],
               start=(ci == 0), stop=(ci == HC - 1), inc=(ci == HC - 1),
               waits=([("sc", psb_evacp[0]), ("sc", psb_evacp[1]),
                       ("vc", v_consts)] if ci == 0 else ()))
        return (s, sl, c("tn"))

    def emit_latent_copy(pend):
        s, sl, t_lat = pend
        return SOP(lambda e, sl=sl, ss=s:
                   e.copy(LATB[:, 512 * ss: 512 * ss + 512], sl),
                   waits=[("tn", t_lat)])

    def emit_routing_a(tt):
        nonlocal expm_read
        expm_read = VOP(lambda e: e.tensor_scalar_mul(SCRS[:, :], EXPM[:, :],
                                                      -1.0),
                        waits=[("sc", s_expm)])
        VOPW(lambda e: e.max(out=T8[:, :], in_=SCRS[:, :]))
        VOPW(lambda e: e.tensor_scalar(RW[:, 0:2], T8[:, 0:2], -1.0, 1.0,
                                       ALU.mult, ALU.add))
        VOPW(lambda e: e.reciprocal(RW[:, 2:3], RW[:, 0:1]))
        VOP(lambda e: e.reciprocal(RW[:, 3:4], RW[:, 1:2]))
        VOPW(lambda e: e.tensor_add(RW[:, 4:5], RW[:, 2:3], RW[:, 3:4]))
        VOPW(lambda e: e.tensor_scalar_add(RW[:, 4:5], RW[:, 4:5], 1e-8))
        VOPW(lambda e: e.reciprocal(RW[:, 5:6], RW[:, 4:5]))
        VOPW(lambda e: e.tensor_tensor(RW[:, 6:8], RW[:, 2:4],
                                       RW[:, 5:6].to_broadcast([128, 2]),
                                       ALU.mult))
        VOPW(lambda e: e.tensor_sub(RW[:, 8:9], RW[:, 6:7], RW[:, 7:8]))

    def emit_routing_b(tt):
        VOP(lambda e, t=tt: e.tensor_tensor(SEL[:, 8 * t: 8 * t + 8], SCRS[:, :],
                                            T8[:, 1:2].to_broadcast([128, 8]),
                                            ALU.is_ge))
        VOP(lambda e: e.tensor_tensor(AUX8[:, :], SCRS[:, :],
                                      T8[:, 0:1].to_broadcast([128, 8]),
                                      ALU.is_ge))
        VOPW(lambda e: e.tensor_tensor(AUX8[:, :], AUX8[:, :],
                                       RW[:, 8:9].to_broadcast([128, 8]),
                                       ALU.mult))
        VOPW(lambda e: e.tensor_tensor(AUX8[:, :], AUX8[:, :],
                                       RW[:, 7:8].to_broadcast([128, 8]),
                                       ALU.add))
        VOPW(lambda e, t=tt: e.tensor_mul(WF[:, 8 * t: 8 * t + 8], AUX8[:, :],
                                          SEL[:, 8 * t: 8 * t + 8]))

    emit_xdma(0)
    emit_sums(0)
    for t in range(NT):
        par = t % 2
        xt = XSTG[:, 1024 * par: 1024 * par + 1024]
        if t + 1 < NT:
            emit_xdma(t + 1)
        VOP(lambda e: e.tensor_scalar_mul(C3[:, :], C1[:, :], 1.0 / H),
            waits=[("sc", s_sq)])
        v_c4_prev = VOP(lambda e: e.tensor_scalar(C4[:, :], C2[:, :], 1.0 / H,
                                                  1e-5, ALU.mult, ALU.add))
        VOP(lambda e: e.tensor_mul(C5[:, :], C3[:, :], C3[:, :]))
        VOPW(lambda e: e.tensor_sub(C4[:, :], C4[:, :], C5[:, :]))
        v_var = VPUB()
        SOP(lambda e: e.activation(C6[:, :], C4[:, :], AF.Ln),
            waits=[("vc", v_var), ("vc", v_xnt_prev)])
        s_rstd = SOP(lambda e: e.activation(C6[:, :], C6[:, :], AF.Exp,
                                            scale=-0.5))
        if t + 1 < NT:
            emit_sums(t + 1)
        if pend_lat is not None:
            s_latcopy = emit_latent_copy(pend_lat)
            pend_lat = None
        if t == 0 and dbg:
            VOP(lambda e: e.tensor_copy(DC0[:, 0:1], C4[:, :]))
            VOP(lambda e: e.tensor_copy(DC0[:, 1:2], C6[:, :]),
                waits=[("sc", s_rstd)])
            VOP(lambda e: e.tensor_copy(DC0[:, 2:3], C3[:, :]))
        xnt = XNT[:, 1024 * par: 1024 * par + 1024]
        v_xnt_prev = VOP(
            lambda e, x=xt, o=xnt: e.tensor_scalar(o, x, C3[:, :], C6[:, :],
                                                   ALU.subtract, ALU.mult),
            waits=[("sc", s_rstd), ("tn", xnt_free[par])])
        x_free[par] = v_xnt_prev
        if t > 0:
            emit_routing_a(t - 1)
            emit_routing_b(t - 1)
        for ci in range(HC):
            TR(PSBP[par][:, 128 * ci: 128 * ci + 128],
               XNT[:, 1024 * par + 128 * ci: 1024 * par + 128 * ci + 128],
               waits=([("vc", v_xnt_prev), ("sc", psb_evacp[par]),
                       ("sone", w_sone), ("sc", s_latcopy)]
                      if ci == 0 else ()))
        xnt_free[par] = c("tn")
        t_trp = c("tn")
        psb_evacp[par] = SOP(
            lambda e, tt=t, p=par: e.copy(
                XN[:, :].rearrange("p (c n) -> p c n", c=HC)[:, :,
                                                            128 * tt: 128 * tt + 128],
                PSBP[p][:, :].rearrange("p (c n) -> p c n", c=HC)),
            waits=[("tn", t_trp)])
        slot = PSD0[:, 8 * par: 8 * par + 8]
        for ci in range(HC):
            MM(slot, XN[:, T * ci + 128 * t: T * ci + 128 * t + 128],
               GW[:, 8 * ci: 8 * ci + 8], start=(ci == 0), stop=False,
               waits=([("sc", psb_evacp[par]), ("sc", lg_free_p[par]),
                       ("vc", v_consts)] if ci == 0 else ()), inc=False)
        MM(slot, ONES_R[:, :], LBB[:, :], start=False, stop=True)
        t_lg = c("tn")
        s_expm = SOP(lambda e, sl=slot: e.activation(EXPM[:, :], sl, AF.Exp,
                                                     scale=-1.0),
                     waits=[("tn", t_lg), ("vc", expm_read)])
        lg_free_p[par] = s_expm
        if t % 4 == 3:
            pend_lat = emit_latent_mm(t // 4)
    emit_routing_a(NT - 1)
    emit_routing_b(NT - 1)
    if pend_lat is not None:
        s_latcopy = emit_latent_copy(pend_lat)
        pend_lat = None

    VOP(lambda e: e.tensor_copy(SELB[:, :], SEL[:, :]))
    VOP(lambda e: e.tensor_copy(WFB[:, :], WF[:, :]))
    v_sb = VPUB()
    t_s1 = MM(PSD4[0:1, 0:128], ONES_C[:, :], SELB[:, :], start=True, stop=True,
              waits=[("vc", v_sb), ("sc", psb_evacp[1])])
    t_s2 = MM(PSD4[0:1, 128:256], ONES_C[:, :], WFB[:, :], start=True, stop=True)
    VOP(lambda e: e.reduce_sum(
        STATS[:, 0:8], PSD4[0:1, 0:128].rearrange("p (t e) -> p e t", e=8),
        axis=AXX), waits=[("tn", t_s2)])
    VOPW(lambda e: e.reduce_sum(
        STATS[:, 8:16], PSD4[0:1, 128:256].rearrange("p (t e) -> p e t", e=8),
        axis=AXX))
    v_stats = VPUB()
    DMA("gpsimd", cci_d[:, :], STATS[:, :], "gcc", waits=[("vc", v_stats)])
    do("gpsimd", lambda e: e.collective_compute(
        "AllReduce", ALU.add, replica_groups=[list(range(8))],
        ins=[cci_d.ap()], outs=[cco_d.ap()]), inc="gc", amt=1,
        waits=[("gcc", c("gcc"))])
    DMA("gpsimd", GSTATS[:, :], cco_d[:, :], "gcc", waits=[("gc", c("gc"))])
    w_gst = c("gcc")
    if dbg:
        DMA("gpsimd", dgst_d[:, :], GSTATS[:, :], "gout", waits=[("gcc", w_gst)])
        DMA("gpsimd", dxn_d[:, :], XN[:, :], "gout",
            waits=[("sc", psb_evacp[0]), ("sc", psb_evacp[1])])
        DMA("gpsimd", dxstg_d[:, :], XSTG[:, :], "gout", waits=[("vc", c("vc"))])
        DMA("gpsimd", dsq_d[:, :], SQ, "gout")
        DMA("gpsimd", dxnt_d[:, :], AOUT[:, 1024:2048].bitcast(BF16), "gout")
        DMA("gpsimd", dscrs_d[:, :], SCRS[:, :], "gout")
        DMA("gpsimd", dsel_d[:, :], SEL[:, :], "gout")
        DMA("gpsimd", dstats_d[:, :], STATS[:, :], "gout", waits=[("vc", v_stats)])
        DMA("gpsimd", dexpm_d[:, :], EXPM[:, :], "gout")
        DMA("gpsimd", dt8_d[:, :], T8[:, :], "gout")
        v_lgc = VOP(lambda e: e.tensor_copy(AUX8[:, :], PSD0[:, 0:8]),
                    waits=[("sc", c("sc"))])
        DMA("gpsimd", dlg_d[:, :], AUX8[:, :], "gout", waits=[("vc", v_lgc)])
        DMA("gpsimd", dc0_d[:, :], DC0[:, :], "gout")

    s_latb = c("sc")
    if dbg:
        DMA("gpsimd", dlat_d[:, :], LATB[:, :], "gout", waits=[("sc", s_latb)])

    s_latb = c("sc")
    ps0_free = [s_latb, s_latb]
    v_pre_k = c("vc")
    for c2 in range(HC):
        for s in range(4):
            par = (c2 * 4 + s) % 2
            sl = PSD0[:, 512 * par: 512 * par + 512]
            MM(sl, WKVU[:, 128 * c2: 128 * c2 + 128],
               LATB[:, 512 * s: 512 * s + 512], start=True, stop=True,
               waits=[("sc", s_latb), ("sc", ps0_free[par]), ("vc", v_pre_k)])
            t_k = c("tn")
            ps0_free[par] = SOP(
                lambda e, sl=sl, cc=c2, ss=s:
                e.copy(KB[:, T * cc + 512 * ss: T * cc + 512 * ss + 512], sl),
                waits=[("tn", t_k)])
            VOP(lambda e, cc=c2, ss=s: e.reduce_max(
                QKM[:, 32 + 4 * cc + ss: 33 + 4 * cc + ss],
                KB[:, T * cc + 512 * ss: T * cc + 512 * ss + 512], axis=AXX,
                apply_absolute_value=True), waits=[("sc", ps0_free[par])])

    v_free = [c("vc"), c("vc")]
    for kc in range(NT):
        par = kc % 2
        pd = PSD1 if par == 0 else PSD2
        for s2 in range(2):
            MM(pd[:, 512 * s2: 512 * s2 + 512],
               LATB[:, 128 * kc: 128 * kc + 128],
               WKVU[:, 1024 + 512 * s2: 1024 + 512 * s2 + 512],
               start=True, stop=True,
               waits=([("sc", s_latb), ("vc", v_free[par]), ("vc", v_vones)]
                      if s2 == 0 else ()))
        t_v = c("tn")
        v_free[par] = VOP(
            lambda e, pd=pd, k=kc: e.tensor_copy(
                VB[:, 1040 * k: 1040 * k + 1040]
                .rearrange("p (h d) -> p h d", d=65)[:, :, 0:64],
                pd[:, :].rearrange("p (h d) -> p h d", d=64)),
            waits=[("tn", t_v)])
    v_vdone = c("vc")
    t_kv_mm = c("tn")

    for qi in (2, 3, 0, 1):
        v_wq = stream_half(
            r3(wq_d[256 * qi: 256 * qi + 256, :]),
            WQB[:, 2048 * qi: 2048 * qi + 2048],
            extra=([("vc", v_consts)] if qi >= 2 else
                   [("vc", v_consts), ("tn", t_kv_mm), ("vc", v_vdone),
                    ("sc", c("sc"))]), ndim3=2)

    ps0q_free = [c("sc"), c("sc")]
    for c2 in range(HC):
        for s in range(4):
            par = (c2 * 4 + s) % 2
            sl = PSD0[:, 512 * par: 512 * par + 512]
            for ci in range(HC):
                MM(sl, WQB[:, 1024 * ci + 128 * c2: 1024 * ci + 128 * c2 + 128],
                   XN[:, T * ci + 512 * s: T * ci + 512 * s + 512],
                   start=(ci == 0), stop=(ci == HC - 1),
                   waits=([("vcast", v_wq), ("sc", ps0q_free[par])]
                          if ci == 0 else ()))
            t_q = c("tn")
            ps0q_free[par] = SOP(
                lambda e, sl=sl, cc=c2, ss=s:
                e.activation(QB[:, T * cc + 512 * ss: T * cc + 512 * ss + 512], sl,
                             AF.Copy, scale=0.125),
                waits=[("tn", t_q)])
            VOP(lambda e, cc=c2, ss=s: e.reduce_max(
                QKM[:, 4 * cc + ss: 4 * cc + ss + 1],
                QB[:, T * cc + 512 * ss: T * cc + 512 * ss + 512], axis=AXX,
                apply_absolute_value=True), waits=[("sc", ps0q_free[par])])
    s_q = c("sc")
    t_qmm = c("tn")
    if dbg:
        DMA("gpsimd", dq_d[:, :], QB[:, :], "gout", waits=[("sc", s_q)])
        DMA("gpsimd", dk_d[:, :], KB[:, :], "gout", waits=[("sc", s_q)])

    for qi in range(4):
        v_wout = stream_half(
            r3(wout_d[256 * qi: 256 * qi + 256, :]),
            WOUTB[:, 2048 * qi: 2048 * qi + 2048],
            extra=[("tn", t_qmm)], ndim3=2)

    VOPW(lambda e: e.reduce_max(QKM[:, 64:65], QKM[:, 0:32], axis=AXX))
    VOPW(lambda e: e.reduce_max(QKM[:, 65:66], QKM[:, 32:64], axis=AXX))
    VOPW(lambda e: e.tensor_copy(QK2[:, :], QKM[:, 64:66]))
    v_qk2 = VPUB()
    t_tr1 = TR(PSB[0:1, 0:128], QK2[:, 0:1],
               waits=[("vc", v_qk2), ("sc", psb_evacp[0])])
    v_mx1 = VOP(lambda e: e.reduce_max(MX2[:, 0:1], PSB[0:1, 0:128], axis=AXX),
                waits=[("tn", t_tr1)])
    t_tr2 = TR(PSB[0:1, 128:256], QK2[:, 1:2], waits=[("vc", v_mx1)])
    VOP(lambda e: e.reduce_max(MX2[:, 1:2], PSB[0:1, 128:256], axis=AXX),
        waits=[("tn", t_tr2)])
    VOPW(lambda e: e.tensor_mul(MR2[:, 0:1], MX2[:, 0:1], MX2[:, 1:2]))
    VOPW(lambda e: e.tensor_scalar(MR2[:, 1:2], MR2[:, 0:1], -64.0, 20.0,
                                   ALU.mult, ALU.add))
    VOPW(lambda e: e.tensor_copy(MR2B[:, :], MR2[:, 1:2]))
    v_negb = VPUB()
    t_negb = MM(PSD4[:, 264:265], ONES_R[:, :], MR2B[:, :], start=True, stop=True,
                waits=[("vc", v_negb), ("vc", v_stats)])
    VOP(lambda e: e.tensor_copy(NEGB[:, :], PSD4[:, 264:265]),
        waits=[("tn", t_negb)])
    v_negb2 = VPUB()
    if dbg:
        DMA("gpsimd", dnegb_d[:, :], NEGB[:, :], "gout", waits=[("vc", v_negb2)])

    barrier()

    PTE = [PT[0], PT[1], AOUT[:, 0:512].bitcast(BF16)]
    PTO = [bfv(OFF_R1, 2048), XSTG[:, 1536:2048].bitcast(BF16),
           AOUT[:, 512:1024].bitcast(BF16)]
    SROW_E = SROW
    SROW_O = AOUT[0:1, 1024:1536].bitcast(BF16)
    sc_free_e = c("sc")
    sc_free_o = c("sc")
    pt_e_free = {0: 0, 1: 0, 2: 0}
    pt_o_free = {0: 0, 1: 0, 2: 0}
    ype_free = c("vc")
    ypo_free = c("vc")
    yq_free = 0
    aout_free = [c("gout"), c("gout")]
    yodd_free = 0
    first_exp = True
    pend_tails = None
    qg_guard = 0
    srow_free_e = 0
    srow_free_o = 0
    t_ave = t_avo = 0

    def emit_tail_a(h, yp, t_av):
        c2h = h // 2
        srw = SROW_E if h % 2 == 0 else SROW_O
        sfree = srow_free_e if h % 2 == 0 else srow_free_o
        SOP(lambda e, yp=yp: e.activation(SROW32, yp[64:65, 0:1024], AF.Ln),
            waits=[("tn", t_av)])
        SOP(lambda e, srw=srw: e.activation(srw, SROW32, AF.Exp, scale=-1.0),
            waits=[("tn", sfree)])
        s_srow = SPUB()
        if h % 2 == 0:
            ydst = YQ[0:64, 1024 * c2h: 1024 * c2h + 1024]
            w_ev = [("tn", t_av), ("tn", yq_free)]
        else:
            ydst = YODD[0:64, :]
            w_ev = [("tn", t_av), ("gyo", yodd_free)]
        VOP(lambda e, yp=yp, yd=ydst: e.tensor_copy(yd, yp[0:64, 0:1024]),
            waits=w_ev)
        v_yev = VPUB()
        return (h, yp, ydst, srw, s_srow, v_yev)

    def emit_tail_b(tb):
        nonlocal yodd_free, srow_free_e, srow_free_o, ype_free, ypo_free
        (h, yp, ydst, srw, s_srow, v_yev) = tb
        for s2 in range(2):
            MM(yp[0:64, 512 * s2: 512 * s2 + 512], ONES_R[:, 0:64],
               srw[:, 512 * s2: 512 * s2 + 512], start=True, stop=True,
               waits=([("sc", s_srow), ("vc", v_yev)] if s2 == 0 else ()))
        t_bc = c("tn")
        if h % 2 == 0:
            srow_free_e = t_bc
        else:
            srow_free_o = t_bc
        VOP(lambda e, yd=ydst, yp=yp: e.tensor_mul(yd, yd, yp[0:64, 0:1024]),
            waits=[("tn", t_bc)])
        v_div = VPUB()
        if h % 2 == 0:
            ype_free = v_div
        else:
            ypo_free = v_div
            yodd_free = DMA("sync",
                            YQ[64:128, 1024 * (h // 2): 1024 * (h // 2) + 1024],
                            YODD[0:64, :], "gyo",
                            waits=[("vc", v_div), ("tn", yq_free)])

    for qg in range(NQG):
        q0g = QGS * qg
        kmax = 8 * (qg + 1)
        for pp in range(NH // 2):
            he, ho = 2 * pp, 2 * pp + 1

            def qkt2(kc, xg_o=()):
                qlo = max(q0g, 128 * kc)
                MM(PSD0[:, 0:512], IDENT[:, :], QB[:, 0:512],
                   start=True, stop=True, inc=False,
                   waits=[("sc", sc_free_e)])
                if qg == 0:
                    MM(PSD0[:, 0:512], IDENT[:, :], QB[:, 0:512],
                       start=True, stop=True, inc=False)
                t_q = {}
                for (rr_, pair, scf, xg2) in ((0, PSD0, sc_free_e, ()),
                                              (64, PSD4, sc_free_o, xg_o)):
                    b0 = qlo
                    first_mm = True
                    while b0 < q0g + QGS:
                        b1 = min(q0g + QGS, (b0 // 512 + 1) * 512)
                        MM(pair[:, b0 - q0g: b1 - q0g],
                           KB[rr_: rr_ + 64,
                              T * pp + 128 * kc: T * pp + 128 * kc + 128],
                           QB[rr_: rr_ + 64, T * pp + b0: T * pp + b1],
                           start=True, stop=True, inc=(b1 == q0g + QGS),
                           waits=(([("sc", scf)] + list(xg2))
                                  if first_mm else ()))
                        first_mm = False
                        b0 = b1
                    t_q[rr_] = c("tn")
                return (kc, qlo, t_q[0], t_q[64])

            def exp2(st):
                nonlocal first_exp, sc_free_e, sc_free_o
                (kc, qlo, t_qe, t_qo) = st
                a0 = qlo - q0g
                w_e = [("tn", t_qe), ("tn", pt_e_free[kc % 3])]
                if first_exp:
                    w_e.append(("vc", v_negb2))
                    first_exp = False
                sc_free_e = SOP(
                    lambda e, k=kc, a=a0: e.activation(
                        PTE[k % 3][:, a: 1024], PSD0[:, a: 1024], AF.Exp,
                        bias=NEGB[:, :]), waits=w_e)
                sc_free_o = SOP(
                    lambda e, k=kc, a=a0: e.activation(
                        PTO[k % 3][:, a: 1024], PSD4[:, a: 1024], AF.Exp,
                        bias=NEGB[:, :]),
                    waits=[("tn", t_qo), ("tn", pt_o_free[kc % 3])])
                s_e, s_o = sc_free_e, sc_free_o
                w_mask = []
                if kc >= 8 * qg:
                    VOP(lambda e, k=kc, a=a0: e.tensor_mul(
                        PTE[k % 3][:, a: a + 128], PTE[k % 3][:, a: a + 128],
                        TRI[:, :]), waits=[("sc", s_e)])
                    VOP(lambda e, k=kc, a=a0: e.tensor_mul(
                        PTO[k % 3][:, a: a + 128], PTO[k % 3][:, a: a + 128],
                        TRI[:, :]), waits=[("sc", s_o)])
                    w_mask = [("vc", VPUB())]
                return (kc, qlo, s_e, s_o, w_mask)

            def av2(st2):
                nonlocal t_ave, t_avo
                (kc, qlo, s_e, s_o, w_mask) = st2
                for (hh, ypx, PTx, s_x, pfx, yfx) in (
                        (he, PSD1, PTE, s_e, pt_e_free, ype_free),
                        (ho, PSD2, PTO, s_o, pt_o_free, ypo_free)):
                    b0 = qlo
                    first_av = True
                    while b0 < q0g + QGS:
                        b1 = min(q0g + QGS, (b0 // 512 + 1) * 512)
                        w_av = []
                        if first_av:
                            w_av += [("sc", s_x)] + w_mask
                            if kc == 0:
                                w_av.append(("vc", yfx))
                        bank = (b0 - q0g) // 512
                        pfx[kc % 3] = MM(
                            ypx[0:65, b0 - q0g: b1 - q0g],
                            VB[:, 1040 * kc + 65 * hh:
                               1040 * kc + 65 * hh + 65],
                            PTx[kc % 3][:, b0 - q0g: b1 - q0g],
                            start=(kc == 0),
                            stop=(kc == 8 * qg + 4 * bank + 3),
                            waits=w_av, inc=(b1 == q0g + QGS))
                        first_av = False
                        b0 = b1
                    if hh == he:
                        t_ave = c("tn")
                    else:
                        t_avo = c("tn")

            xg_o = [("vc", qg_guard)] if pp == 0 else []
            sts = {0: qkt2(0, xg_o)}
            exps = {}
            ta_e = ta_o = None
            for kc in range(1, kmax):
                exps[kc - 1] = exp2(sts.pop(kc - 1))
                sts[kc] = qkt2(kc)
                if kc == 1 and pend_tails is not None:
                    ta_e = emit_tail_a(*pend_tails[0])
                if kc == 2 and pend_tails is not None:
                    ta_o = emit_tail_a(*pend_tails[1])
                    emit_tail_b(ta_e)
                    ta_e = None
                if kc == 3 and ta_o is not None:
                    emit_tail_b(ta_o)
                    ta_o = None
                    pend_tails = None
                if kc - 3 in exps:
                    av2(exps.pop(kc - 3))
            exps[kmax - 1] = exp2(sts.pop(kmax - 1))
            for k in sorted(exps):
                av2(exps.pop(k))
            pend_tails = [(he, PSD1, t_ave), (ho, PSD2, t_avo)]
        t_av_all = c("tn")
        if pend_tails is not None:
            tfa = emit_tail_a(*pend_tails[0])
            tfb = emit_tail_a(*pend_tails[1])
            emit_tail_b(tfa)
            emit_tail_b(tfb)
            pend_tails = None
        w_op = [("gyo", c("gyo")), ("vc", c("vc")), ("vcast", v_wout)]
        po_free = 0
        for tcl in range(8):
            tglob = 8 * qg + tcl
            for s2 in range(2):
                for c2 in range(HC):
                    MM(PSD4[:, 0:512],
                       YQ[:, 1024 * c2 + 128 * tcl: 1024 * c2 + 128 * tcl + 128],
                       WOUTB[:, 1024 * c2 + 512 * s2: 1024 * c2 + 512 * s2 + 512],
                       start=(c2 == 0), stop=(c2 == HC - 1), inc=(c2 == HC - 1),
                       waits=(w_op + [("vc", po_free)]) if c2 == 0 else ())
                    w_op = []
                t_po = c("tn")
                po_free = VOP(
                    lambda e, par=tcl % 2, s2=s2:
                    e.tensor_copy(AOUT[:, 1024 * par + 512 * s2:
                                       1024 * par + 512 * s2 + 512], PSD4[:, 0:512]),
                    waits=[("tn", t_po)] +
                          ([("gout", aout_free[tcl % 2])] if s2 == 0 else []))
            v_po = VPUB()
            aout_free[tcl % 2] = DMA(
                "gpsimd", attn_d[128 * tglob: 128 * tglob + 128, :],
                AOUT[:, 1024 * (tcl % 2): 1024 * (tcl % 2) + 1024], "gout",
                waits=[("vc", v_po)])
        yq_free = c("tn")
        qg_guard = c("vc")
    g_attn = c("gout")

    VOP(lambda e: e.tensor_scalar_max(GSTATS[:, 0:8], GSTATS[:, 0:8], 1.0),
        waits=[("gcc", w_gst)])
    v_cnt = VPUB()
    SOP(lambda e: e.activation(GLN[:, :], GSTATS[:, 0:8], AF.Ln),
        waits=[("vc", v_cnt)])
    SOP(lambda e: e.activation(GLN[:, :], GLN[:, :], AF.Exp, scale=-1.0))
    s_gln = SPUB()
    VOP(lambda e: e.tensor_mul(WMB[:, :], GSTATS[:, 8:16], GLN[:, :]),
        waits=[("sc", s_gln)])
    v_wmb = VPUB()

    nslot[0] = 4
    stg_gate[2] = [("tn", yq_free), ("gout", g_attn)]
    stg_gate[3] = [("tn", srow_free_e), ("sc", c("sc"))]
    wpar[0] += (-wpar[0]) % 4
    pre_snap = [(s, v) for s, v in cnt.items() if v > 0]
    ewr_g = {}
    ewr_u = {}
    ewr_d = {}
    ew_users = {0: 0, 1: 0}

    def stream_mat(dr, e_, dst, extra=()):
        rows = dr.shape[1]
        for hf in range(2):
            hr = rows // 2
            src = r3(dr[e_, hr * hf: hr * hf + hr, :])
            r = stream_half(src, dst[:, 2048 * hf: 2048 * hf + 2048],
                            extra, ndim3=hr // 128)
        return r

    ewr_g[0] = stream_mat(egwT_d, 0, EW[0][0], [("tn", t_av_all)])
    ewr_u[0] = stream_mat(eupT_d, 0, EW[0][1])
    ewr_d[0] = stream_mat(edwT_d, 0, EW[0][2])
    ewr_g[1] = stream_mat(egwT_d, 1, EW[1][0], [("tn", yq_free)])
    ewr_u[1] = stream_mat(eupT_d, 1, EW[1][1])
    ewr_d[1] = stream_mat(edwT_d, 1, EW[1][2])

    barrier(pre_snap)

    v_wmean = 0
    sg_ready = {}
    su_ready = {}
    sgc_users = {0: 0, 1: 0}

    def col_chunk(dr, si):
        return r3(dr[:, 128 * si: 128 * si + 128])

    def stream_gu(si, extra=()):
        p = si % 2
        sg_ready[si] = stream_half(col_chunk(sgwT_d, si), SGC[p][:, 0:1024],
                                   list(extra) + [("tn", sgc_users[p])],
                                   ncols=1024, ndim3=8)
        su_ready[si] = stream_half(col_chunk(supT_d, si), SUC[p][:, 0:1024],
                                   list(extra) + [("tn", sgc_users[p])],
                                   ncols=1024, ndim3=8)

    gu_free = {"g0": 0, "g1": 0}
    gu_u = [0, 0]
    down_free = [0, 0]
    acc_done = [0] * NT

    for e_ in range(E):
        p = e_ % 2
        GTb, UTb, DTb = EW[p]
        w_w = [("vcast", ewr_g[e_])]
        g_users = 0
        for ic in range(4):
            for hh in range(HC):
                for s in range(4):
                    pd = PSD1 if s < 2 else PSD2
                    sl = pd[:, 512 * (s % 2): 512 * (s % 2) + 512]
                    MM(sl, GTb[:, 512 * hh + 128 * ic: 512 * hh + 128 * ic + 128],
                       XN[:, T * hh + 512 * s: T * hh + 512 * s + 512],
                       start=(hh == 0), stop=(hh == HC - 1),
                       inc=(hh == HC - 1 and s == 3),
                       waits=(w_w + [("sc", gu_free["g0"]), ("sc", gu_free["g1"])])
                       if (hh == 0 and s == 0) else ())
                    w_w = []
            t_g = c("tn")
            if ic == 3:
                g_users = t_g
            for s in range(4):
                pd = PSD1 if s < 2 else PSD2
                sl = pd[:, 512 * (s % 2): 512 * (s % 2) + 512]
                k = SOP(lambda e, sl=sl, s=s: e.activation(
                    SGE[:, 512 * s: 512 * s + 512], sl, AF.Silu),
                    waits=([("tn", t_g), ("vc", gu_u[0]), ("vc", gu_u[1])]
                           if s == 0 else [("tn", t_g)]))
                if s == 1:
                    gu_free["g0"] = k
                if s == 3:
                    gu_free["g1"] = k
            s_silu = c("sc")
            for hp2 in range(2):
                updn = PSD0 if hp2 == 0 else PSD4
                for hh in range(HC):
                    for s in (2 * hp2, 2 * hp2 + 1):
                        sl = updn[:, 512 * (s % 2): 512 * (s % 2) + 512]
                        MM(sl, UTb[:, 512 * hh + 128 * ic: 512 * hh + 128 * ic + 128],
                           XN[:, T * hh + 512 * s: T * hh + 512 * s + 512],
                           start=(hh == 0), stop=(hh == HC - 1),
                           inc=(hh == HC - 1 and s == 2 * hp2 + 1),
                           waits=([("vc", gu_u[hp2]), ("vc", down_free[hp2])] +
                                  ([("vcast", ewr_u[e_])]
                                   if (ic == 0 and hp2 == 0) else [])
                                  if (hh == 0 and s == 2 * hp2) else ()))
                t_u = c("tn")
                for s in (2 * hp2, 2 * hp2 + 1):
                    sl = updn[:, 512 * (s % 2): 512 * (s % 2) + 512]
                    gu_u[hp2] = VOP(
                        lambda e, sl=sl, s=s, ic=ic: e.tensor_mul(
                            H_CH[ic][:, 512 * s: 512 * s + 512],
                            SGE[:, 512 * s: 512 * s + 512], sl),
                        waits=[("tn", t_u), ("sc", s_silu)])
        u_users = c("tn")
        if e_ == 0:
            t_wmb = MM(PSD4[:, 256:264], ONES_R[:, :], WMB[:, :], start=True,
                       stop=True, waits=[("vc", v_wmb), ("vc", gu_u[1])])
            VOP(lambda e: e.tensor_copy(WMEANB[:, :], PSD4[:, 256:264]),
                waits=[("tn", t_wmb)])
            v_wmean = VPUB()
            down_free[1] = v_wmean
        v_h = c("vc")
        for tc in range(NT):
            pd = PSD0 if tc % 2 == 0 else PSD4
            for ic in range(4):
                for s2 in range(2):
                    MM(pd[:, 512 * s2: 512 * s2 + 512],
                       H_CH[ic][:, 128 * tc: 128 * tc + 128],
                       DTb[:, 1024 * ic + 512 * s2: 1024 * ic + 512 * s2 + 512],
                       start=(ic == 0), stop=(ic == 3),
                       inc=(ic == 3 and s2 == 1),
                       waits=([("vc", v_h), ("vc", down_free[tc % 2]),
                               ("vc", gu_u[tc % 2])] +
                              ([("vcast", ewr_d[e_])] if tc == 0 else [])
                              if (s2 == 0 and ic == 0) else ()))
            t_dn = c("tn")
            if tc == 6 and e_ + 2 < E:
                ewr_g[e_ + 2] = stream_mat(egwT_d, e_ + 2, GTb,
                                           [("tn", g_users)])
                ewr_u[e_ + 2] = stream_mat(eupT_d, e_ + 2, UTb,
                                           [("tn", u_users)])
            v_ws = VOP(lambda e, tc=tc, ee=e_: e.tensor_mul(
                WS[:, :], SEL[:, 8 * tc + ee: 8 * tc + ee + 1],
                WMEANB[:, ee: ee + 1]),
                waits=([("vc", v_wmean)] if (e_ == 0 and tc == 0) else ()))
            if e_ == 0:
                down_free[tc % 2] = VOP(
                    lambda e, pd=pd, tc=tc: e.tensor_scalar(
                        ACC[:, 1024 * tc: 1024 * tc + 1024], pd[:, 0:1024],
                        WS[:, :], None, ALU.mult),
                    waits=[("tn", t_dn), ("vc", v_ws)] +
                          ([("tn", yq_free), ("gout", g_attn)] if tc == 0 else []))
            else:
                down_free[tc % 2] = VOP(
                    lambda e, pd=pd, tc=tc: e.scalar_tensor_tensor(
                        ACC[:, 1024 * tc: 1024 * tc + 1024], pd[:, 0:1024], WS[:, :],
                        ACC[:, 1024 * tc: 1024 * tc + 1024], ALU.mult, ALU.add),
                    waits=[("tn", t_dn), ("vc", v_ws)])
            acc_done[tc] = down_free[tc % 2]
        ew_users[p] = c("tn")
        if e_ + 2 < E:
            ewr_d[e_ + 2] = stream_mat(edwT_d, e_ + 2, DTb,
                                       [("tn", ew_users[p])])
        if e_ == E - 1:
            t_e7dn = [("tn", c("tn"))]
            stream_gu(0, t_e7dn)
            stream_gu(1, t_e7dn)

    gu_s = {"g0": c("sc"), "g1": 0}
    gu_su = [c("vc"), c("vc")]
    sd_ready = 0
    sdown_free = [c("vc"), c("vc")]
    hs_done = 0
    for si in range(8):
        p = si % 2
        for hh in range(HC):
            for s in range(4):
                pd = PSD1 if s < 2 else PSD2
                sl = pd[:, 512 * (s % 2): 512 * (s % 2) + 512]
                MM(sl, SGC[p][:, 128 * hh: 128 * hh + 128],
                   XN[:, T * hh + 512 * s: T * hh + 512 * s + 512],
                   start=(hh == 0), stop=(hh == HC - 1),
                   inc=(hh == HC - 1 and s == 3),
                   waits=([("vcast", sg_ready[si]),
                           ("sc", gu_s["g0"]), ("sc", gu_s["g1"]),
                           ("vc", sdown_free[0]), ("vc", sdown_free[1])]
                          if (hh == 0 and s == 0) else ()))
        t_g = c("tn")
        for s in range(4):
            pd = PSD1 if s < 2 else PSD2
            sl = pd[:, 512 * (s % 2): 512 * (s % 2) + 512]
            k = SOP(lambda e, sl=sl, s=s: e.activation(
                SGS[:, 512 * s: 512 * s + 512], sl, AF.Silu),
                waits=([("tn", t_g), ("vc", gu_su[0]), ("vc", gu_su[1])]
                       if s == 0 else [("tn", t_g)]))
            if s == 1:
                gu_s["g0"] = k
            if s == 3:
                gu_s["g1"] = k
        s_silu = c("sc")
        for hp2 in range(2):
            updn = PSD0 if hp2 == 0 else PSD4
            for hh in range(HC):
                for s in (2 * hp2, 2 * hp2 + 1):
                    sl = updn[:, 512 * (s % 2): 512 * (s % 2) + 512]
                    MM(sl, SUC[p][:, 128 * hh: 128 * hh + 128],
                       XN[:, T * hh + 512 * s: T * hh + 512 * s + 512],
                       start=(hh == 0), stop=(hh == HC - 1),
                       inc=(hh == HC - 1 and s == 2 * hp2 + 1),
                       waits=([("vcast", su_ready[si]), ("vc", gu_su[hp2])]
                              if (hh == 0 and s == 2 * hp2) else ()))
            t_u = c("tn")
            for s in (2 * hp2, 2 * hp2 + 1):
                sl = updn[:, 512 * (s % 2): 512 * (s % 2) + 512]
                gu_su[hp2] = VOP(
                    lambda e, sl=sl, s=s, si=si: e.tensor_mul(
                        HS_CH[si][:, 512 * s: 512 * s + 512],
                        SGS[:, 512 * s: 512 * s + 512], sl),
                    waits=[("tn", t_u), ("sc", s_silu)])
        sgc_users[p] = c("tn")
        if si + 2 < 8:
            stream_gu(si + 2)
        sd_ready = stream_half(
            sdwT_d[128 * si: 128 * si + 128, :],
            SD[:, 1024 * si: 1024 * si + 1024],
            extra=[("tn", ew_users[0])], ncols=1024)
    v_hs = c("vc")

    shared_done = [0] * NT
    at_free = [c("vc"), c("vc")]

    def emit_p5(tc):
        par = tc % 2
        sxn = "sxA" if par == 0 else "sxB"
        at = ACC[:, 1024 * tc: 1024 * tc + 1024]
        stg = AOUT[:, 1024 * par: 1024 * par + 1024]
        xstg = XSTG[:, 1024 * par: 1024 * par + 1024]
        DMA("sync", stg, attn_d[128 * tc: 128 * tc + 128, :], sxn,
            waits=[("gout", g_attn), ("vc", at_free[par]),
                   ("vcast", w32_free[2]), ("vcast", w32_free[3])])
        DMA("sync", xstg, x_d[128 * tc: 128 * tc + 128, :], sxn,
            waits=[("vcast", w32_free[3])])
        w_at = c(sxn)
        VOP(lambda e, at=at, stg=stg: e.tensor_add(at, at, stg),
            waits=[(sxn, w_at), ("vc", shared_done[tc])])
        VOP(lambda e, at=at, xs=xstg: e.tensor_add(at, at, xs))
        at_free[par] = v_fin = VPUB()
        DMA("gpsimd", out_d[128 * tc: 128 * tc + 128, :], at, "sout",
            waits=[("vc", v_fin)])

    for tc in range(NT):
        pd = PSD1 if tc % 2 == 0 else PSD2
        for si in range(8):
            for s2 in range(2):
                MM(pd[:, 512 * s2: 512 * s2 + 512],
                   HS_CH[si][:, 128 * tc: 128 * tc + 128],
                   SD[:, 1024 * si + 512 * s2: 1024 * si + 512 * s2 + 512],
                   start=(si == 0), stop=(si == 7),
                   inc=(si == 7 and s2 == 1),
                   waits=([("vc", v_hs), ("vc", sdown_free[tc % 2]),
                           ("vcast", sd_ready)]
                          if (s2 == 0 and si == 0) else ()))
        t_dn = c("tn")
        VOP(lambda e, pd=pd, tc=tc: e.tensor_tensor(
                ACC[:, 1024 * tc: 1024 * tc + 1024],
                ACC[:, 1024 * tc: 1024 * tc + 1024], pd[:, :], ALU.add),
            waits=[("tn", t_dn), ("vc", acc_done[tc])])
        sdown_free[tc % 2] = VPUB()
        shared_done[tc] = sdown_free[tc % 2]
        emit_p5(tc)
    if dbg:
        DMA("gpsimd", dacc_d[:, :], ACC[:, :], "gout",
            waits=[("vc", max(shared_done))])

    sched["gpsimd"].append(([("sout", c("sout"))], None, None, 0))

    with es:
        with nc.Block() as block:
            def runner(name):
                def _run(eng):
                    for waits, fn, inc, amt in sched[name]:
                        for (s, v) in waits:
                            eng.wait_ge(sems[s], v)
                        if fn is not None:
                            ins = fn(eng)
                            if inc is not None:
                                ins.then_inc(sems[inc], amt)
                return _run
            block.sync(runner("sync"))
            block.gpsimd(runner("gpsimd"))
            block.tensor(runner("tensor"))
            block.vector(runner("vector"))
            block.scalar(runner("scalar"))
    return nc


def prep_inputs(inputs):
    f = np.float32
    g = {}
    g["wq"] = np.ascontiguousarray(np.asarray(inputs["L0_attn_w_q"], f))
    g["wout"] = np.ascontiguousarray(np.asarray(inputs["L0_attn_w_out"], f))
    g["wkvd"] = np.ascontiguousarray(np.asarray(inputs["L0_attn_w_kv_down"], f))
    g["wkvu"] = np.ascontiguousarray(np.asarray(inputs["L0_attn_w_kv_up"], f))
    g["gwT"] = np.ascontiguousarray(np.asarray(inputs["L0_ffn_gate_w"], f).T)
    g["lb"] = np.ascontiguousarray(np.asarray(inputs["L0_ffn_lb_bias"], f)
                                   .reshape(1, 8))
    g["egwT"] = np.ascontiguousarray(
        np.asarray(inputs["L0_ffn_expert_gate_w"], f).transpose(0, 2, 1))
    g["eupT"] = np.ascontiguousarray(
        np.asarray(inputs["L0_ffn_expert_up_w"], f).transpose(0, 2, 1))
    g["edwT"] = np.ascontiguousarray(
        np.asarray(inputs["L0_ffn_expert_down_w"], f).transpose(0, 2, 1))
    g["sgwT"] = np.ascontiguousarray(np.asarray(inputs["L0_ffn_shared_gate_w"], f).T)
    g["supT"] = np.ascontiguousarray(np.asarray(inputs["L0_ffn_shared_up_w"], f).T)
    g["sdwT"] = np.ascontiguousarray(np.asarray(inputs["L0_ffn_shared_down_w"], f).T)
    g["ident"] = np.eye(128, dtype=BFNP)
    r = np.arange(128)
    g["tri"] = (r[:, None] <= r[None, :]).astype(BFNP)
    return g


def kernel(**inputs):
    x = np.ascontiguousarray(np.asarray(inputs["L0_attn_x"], np.float32))
    g = prep_inputs(inputs)
    nc = build_program()
    in_maps = [dict(g, x=np.ascontiguousarray(x[b])) for b in range(8)]
    res = run_bass_kernel_spmd(nc, in_maps, core_ids=list(range(8)))
    return np.stack([r["out"] for r in res.results], axis=0)


# version 31 (final)
# speedup vs baseline: 1.0121x; 1.0121x over previous
import numpy as np
import ml_dtypes
import concourse.bass as bass
import concourse.bass_utils as _bu

if not getattr(_bu, "_ldwopt_patched", False):
    _orig_run_command = _bu.run_command

    def _run_command_ldwopt(argv, **kwargs):
        argv = ["--enable-ldw-opt=true" if a == "--enable-ldw-opt=false" else a
                for a in argv]
        return _orig_run_command(argv, **kwargs)

    _bu.run_command = _run_command_ldwopt
    _bu._ldwopt_patched = True
import concourse.mybir as mybir
from concourse.bass_utils import run_bass_kernel_spmd
from contextlib import ExitStack

F32 = mybir.dt.float32
BF16 = mybir.dt.bfloat16
AF = mybir.ActivationFunctionType
ALU = mybir.AluOpType
AXX = mybir.AxisListType.X

T = 2048
NT = 16
HC = 8
H = 1024
E = 8
NH = 16
QGS = 1024
NQG = 2
SI = 1024

BFNP = np.dtype(ml_dtypes.bfloat16)


def build_program(dbg=False):
    nc = bass.Bass("TRN2", target_bir_lowering=False)

    x_d = nc.dram_tensor("x", [T, H], F32, kind="ExternalInput")
    wq_d = nc.dram_tensor("wq", [H, H], F32, kind="ExternalInput")
    wout_d = nc.dram_tensor("wout", [H, H], F32, kind="ExternalInput")
    wkvd_d = nc.dram_tensor("wkvd", [H, 128], F32, kind="ExternalInput")
    wkvu_d = nc.dram_tensor("wkvu", [128, 2 * H], F32, kind="ExternalInput")
    gwT_d = nc.dram_tensor("gwT", [H, E], F32, kind="ExternalInput")
    lb_d = nc.dram_tensor("lb", [1, E], F32, kind="ExternalInput")
    egwT_d = nc.dram_tensor("egwT", [E, H, 512], F32, kind="ExternalInput")
    eupT_d = nc.dram_tensor("eupT", [E, H, 512], F32, kind="ExternalInput")
    edwT_d = nc.dram_tensor("edwT", [E, 512, H], F32, kind="ExternalInput")
    sgwT_d = nc.dram_tensor("sgwT", [H, SI], F32, kind="ExternalInput")
    supT_d = nc.dram_tensor("supT", [H, SI], F32, kind="ExternalInput")
    sdwT_d = nc.dram_tensor("sdwT", [SI, H], F32, kind="ExternalInput")
    ident_d = nc.dram_tensor("ident", [128, 128], BF16, kind="ExternalInput")
    tri_d = nc.dram_tensor("tri", [128, 128], BF16, kind="ExternalInput")
    out_d = nc.dram_tensor("out", [T, H], F32, kind="ExternalOutput")
    attn_d = nc.dram_tensor("attn_sc", [T, H], F32,
                            kind="ExternalOutput" if dbg else "Internal")
    cci_d = nc.dram_tensor("cci", [1, 16], F32)
    cco_d = nc.dram_tensor("cco", [1, 16], F32)
    if dbg:
        dxstg_d = nc.dram_tensor("d_xstg", [128, 2048], F32, kind="ExternalOutput")
        dsq_d = nc.dram_tensor("d_sq", [128, 1024], F32, kind="ExternalOutput")
        dxnt_d = nc.dram_tensor("d_xnt", [128, 2048], BF16, kind="ExternalOutput")
        dscrs_d = nc.dram_tensor("d_scrs", [128, 8], F32, kind="ExternalOutput")
        dsel_d = nc.dram_tensor("d_sel", [128, 128], F32, kind="ExternalOutput")
        dstats_d = nc.dram_tensor("d_stats", [1, 16], F32, kind="ExternalOutput")
        dexpm_d = nc.dram_tensor("d_expm", [128, 8], F32, kind="ExternalOutput")
        dt8_d = nc.dram_tensor("d_t8", [128, 8], F32, kind="ExternalOutput")
        dlg_d = nc.dram_tensor("d_lg", [128, 8], F32, kind="ExternalOutput")
        dc0_d = nc.dram_tensor("d_c0", [128, 4], F32, kind="ExternalOutput")
        dy_d = nc.dram_tensor("d_y", [128, 8192], BF16, kind="ExternalOutput")
        daccf_d = nc.dram_tensor("d_accf", [128, NT * H], F32, kind="ExternalOutput")
        dxd_d = nc.dram_tensor("d_xd", [128, 1024], F32, kind="ExternalOutput")
        dstg_d = nc.dram_tensor("d_stg", [128, 2048], F32, kind="ExternalOutput")
        dxstg2_d = nc.dram_tensor("d_xstg2", [128, 2048], F32, kind="ExternalOutput")
        dxn_d = nc.dram_tensor("d_xn", [128, HC * T], BF16, kind="ExternalOutput")
        dgst_d = nc.dram_tensor("d_gst", [1, 16], F32, kind="ExternalOutput")
        dnegb_d = nc.dram_tensor("d_negb", [128, 1], F32, kind="ExternalOutput")
        dlat_d = nc.dram_tensor("d_lat", [128, T], BF16, kind="ExternalOutput")
        dq_d = nc.dram_tensor("d_q", [128, HC * T], BF16, kind="ExternalOutput")
        dk_d = nc.dram_tensor("d_k", [128, HC * T], BF16, kind="ExternalOutput")
        dacc_d = nc.dram_tensor("d_acc", [128, NT * H], F32, kind="ExternalOutput")

    es = ExitStack()

    def sb(name, shape, dt):
        return es.enter_context(nc.sbuf_tensor(name, shape, dt))

    IDENT = sb("IDENT", [128, 128], BF16)
    TRI = sb("TRI", [128, 128], BF16)
    ONES_R = sb("ONES_R", [1, 128], BF16)
    ONES_C = sb("ONES_C", [128, 1], BF16)
    GW = sb("GW", [128, 64], BF16)
    LBB = sb("LBB", [1, 8], BF16)
    XN = sb("XN", [128, HC * T], BF16)
    SEL = sb("SEL", [128, 128], F32)
    SELW = sb("SELW", [128, 16], BF16)
    SCRS = sb("SCRS", [128, 8], F32)
    EXPM = sb("EXPM", [128, 8], F32)
    T8 = sb("T8", [128, 8], F32)
    AUX8 = sb("AUX8", [128, 8], F32)
    C1 = sb("C1", [128, 1], F32)
    C2 = sb("C2", [128, 1], F32)
    C3 = sb("C3", [128, 1], F32)
    C4 = sb("C4", [128, 1], F32)
    C5 = sb("C5", [128, 1], F32)
    C6 = sb("C6", [128, 1], F32)
    RW = sb("RW", [128, 10], F32)
    GLN = sb("GLN", [1, 8], F32)
    DC0 = sb("DC0", [128, 4], F32)
    DJV = sb("DJV", [128, 8], F32)
    DJS = sb("DJS", [128, 8], F32)
    WF = sb("WF", [128, 128], F32)
    SELB = sb("SELB", [128, 128], BF16)
    WFB = sb("WFB", [128, 128], BF16)
    STATS = sb("STATS", [1, 16], F32)
    GSTATS = sb("GSTATS", [1, 16], F32)
    WMB = sb("WMB", [1, 8], BF16)
    WMEANB = sb("WMEANB", [128, 8], F32)
    NEGB = sb("NEGB", [128, 1], F32)
    QKM = sb("QKM", [128, 66], F32)
    QK2 = sb("QK2", [128, 2], BF16)
    MX2 = sb("MX2", [1, 2], F32)
    MR2 = sb("MR2", [1, 2], F32)
    MR2B = sb("MR2B", [1, 1], BF16)
    XSTG = sb("XSTG", [128, 2 * H], F32)
    W32 = sb("W32", [128, 4096], F32)
    AOUT = sb("AOUT", [128, 2 * H], F32)
    SQ = AOUT[:, 0:1024]
    XNT = AOUT[:, 1024:2048].bitcast(BF16)
    SROW32 = XSTG[0:1, 0:1024]
    SROW = XSTG[0:1, 1024:1536].bitcast(BF16)
    GW32 = W32[:, 3072:3136]
    LBB32 = W32[0:1, 3136:3144]
    YODD = sb("YODD", [64, 1024], BF16)
    WS = sb("WS", [128, 1], F32)

    AR = sb("AR", [128, 70144], BF16)

    def bfv(off_b, nb):
        return AR[:, off_b // 2: (off_b + nb) // 2]

    def r3(ap):
        return ap.rearrange("(c p) n -> p c n", p=128)

    def d3(ap, c_):
        return ap.rearrange("p (c n) -> p c n", c=c_)

    OFF_R1, OFF_PT, OFF_Q, OFF_K, OFF_V, OFF_Y = (
        0, 20 * 1024, 24 * 1024, 56 * 1024, 88 * 1024, 121 * 1024)

    WKVD = bfv(OFF_R1, 2048)
    WKVU = bfv(OFF_R1 + 2048, 4096)
    LATB = bfv(OFF_R1 + 2048 + 4096, 4096)
    WQB = bfv(OFF_R1 + 2048, 16384)
    WOUTB = WQB
    PT = [bfv(OFF_PT, 2048), bfv(OFF_PT + 2048, 2048)]
    QB = bfv(OFF_Q, 32768)
    KB = bfv(OFF_K, 32768)
    VB = bfv(OFF_V, 33280)
    YQ = bfv(OFF_Y, 16384)
    ACC = AR[:, OFF_Q // 2: (OFF_Q + 65536) // 2].bitcast(F32)
    EW = {0: (bfv(OFF_V, 8192), bfv(OFF_V + 8192, 8192), bfv(OFF_V + 16384, 8192)),
          1: (bfv(OFF_Y, 8192), bfv(OFF_Y + 8192, 8192), bfv(OFF_R1 + 12288, 8192))}
    H_CH = [bfv(OFF_R1, 4096), bfv(OFF_R1 + 4096, 4096),
            bfv(OFF_R1 + 8192, 4096), bfv(OFF_PT, 4096)]
    SGE = bfv(OFF_V + 24576, 4096)
    SD = bfv(OFF_V, 16384)
    HS_CH = [bfv(OFF_V + 16384 + 4096 * i, 4096) for i in range(4)] + \
            [bfv(OFF_Y + 4096 * i, 4096) for i in range(4)]
    SGC = [bfv(OFF_R1, 2048), bfv(OFF_R1 + 2048, 2048)]
    SUC = [bfv(OFF_R1 + 4096, 2048), bfv(OFF_R1 + 6144, 2048)]
    SGS = bfv(OFF_R1 + 8192, 4096)

    PSD0 = es.enter_context(nc.psum_tensor("PSD0", [128, 1024], F32))
    PSD1 = es.enter_context(nc.psum_tensor("PSD1", [128, 1024], F32))
    PSD2 = es.enter_context(nc.psum_tensor("PSD2", [128, 1024], F32))
    PSD4 = es.enter_context(nc.psum_tensor("PSD4", [128, 1024], F32))
    PSB = PSD2[:, 0:512].bitcast(BF16)

    sem_names = ["sxA", "sxB", "swA", "swB", "swC", "swD", "gwA", "gwB",
                 "gwC", "gwD", "gxA", "gxB", "gyo", "sone", "gout", "gcc",
                 "gacc", "sout", "tn", "vc", "sc", "vcast", "gc"]
    sems = {n: es.enter_context(nc.semaphore(n)) for n in sem_names}
    sched = {"sync": [], "gpsimd": [], "tensor": [], "vector": [], "scalar": []}
    cnt = {n: 0 for n in sem_names}

    def do(eng, fn, inc=None, amt=1, waits=()):
        w = [(s, v) for (s, v) in waits if v > 0]
        sched[eng].append((w, fn, inc, amt))
        if inc is not None:
            cnt[inc] += amt
            return cnt[inc]
        return None

    def c(s):
        return cnt[s]

    def barrier(snap=None):
        if snap is None:
            snap = [(s, v) for s, v in cnt.items() if v > 0]
        for eng in sched:
            sched[eng].append((snap, None, None, 0))

    def MM(out, lhsT, rhs, start, stop, waits=(), inc=True):
        return do("tensor",
                  lambda e, o=out, l=lhsT, r=rhs, st=start, sp=stop:
                  e.matmul(o, l, r, start=st, stop=sp),
                  inc=("tn" if inc else None), waits=waits)

    def TR(out, in_, waits=()):
        return do("tensor", lambda e, o=out, i=in_: e.transpose(o, i, IDENT[:, :]),
                  inc="tn", waits=waits)

    def DMA(q, out, in_, inc, waits=(), accum=None):
        if accum is None:
            return do(q, lambda e, o=out, i=in_: e.dma_start(out=o, in_=i),
                      inc=inc, amt=16, waits=waits)
        return do(q, lambda e, o=out, i=in_, a=accum:
                  e.dma_start(out=o, in_=i, accum_op=a), inc=inc, amt=16, waits=waits)

    def VOP(fn, waits=(), inc="vc"):
        return do("vector", fn, inc=inc, waits=waits)

    def SOP(fn, waits=(), inc="sc"):
        return do("scalar", fn, inc=inc, waits=waits)

    wpar = [0]
    w32_free = [0, 0, 0, 0]
    nslot = [2]
    stg_gate = {}
    stg_sem = [("swA", "gwA"), ("swB", "gwB"), ("swC", "gwC"), ("swD", "gwD")]
    STG = [W32[:, 0:2048], W32[:, 2048:4096], AOUT, XSTG]

    def VOPW(fn, waits=(), inc="vc"):
        return do("vector", fn, inc=inc, waits=list(waits) + [("vc", c("vc"))])

    def VPUB():
        return VOP(lambda e: e.memset(DJV[:, :], 0.0),
                   waits=[("vc", c("vc"))])

    def SPUB():
        return SOP(lambda e: e.memzero(DJS[:, :]))

    def stream_half(src_ap, dst_ap, extra=(), ncols=2048, ndim3=None):
        p = wpar[0] % nslot[0]
        wpar[0] += 1
        swn, gwn = stg_sem[p]
        w32dst = STG[p][:, 0:ncols]
        wts = [("vcast", w32_free[p])] + list(extra) + stg_gate.pop(p, [])
        if ndim3 is not None and ndim3 % 2 == 0:
            w3 = d3(w32dst, ndim3)
            DMA("sync", w3[:, 0: ndim3 // 2, :], src_ap[:, 0: ndim3 // 2, :],
                swn, waits=wts)
            DMA("gpsimd", w3[:, ndim3 // 2: ndim3, :],
                src_ap[:, ndim3 // 2: ndim3, :], gwn, waits=wts)
        elif ndim3 is not None:
            DMA("sync", d3(w32dst, ndim3), src_ap, swn, waits=wts)
        else:
            DMA("sync", w32dst[:, 0: ncols // 2], src_ap[:, 0: ncols // 2],
                swn, waits=wts)
            DMA("gpsimd", w32dst[:, ncols // 2: ncols],
                src_ap[:, ncols // 2: ncols], gwn, waits=wts)
        w32_free[p] = do("vector",
                         lambda e, p=p, d=dst_ap, n=ncols: e.tensor_copy(
                             d, STG[p][:, 0:n]),
                         inc="vcast", waits=[(swn, c(swn)), (gwn, c(gwn))])
        return c("vcast")

    DMA("sync", IDENT[:, :], ident_d[:, :], "sone")
    DMA("sync", TRI[:, :], tri_d[:, :], "sone")
    DMA("sync", d3(GW32[:, :], 8), r3(gwT_d[:, :]), "sone")
    DMA("sync", LBB32[:, :], lb_d[:, :], "sone")
    DMA("sync", d3(W32[:, 0:1024], 8), r3(wkvd_d[:, :]), "sone")
    DMA("sync", W32[:, 1024:3072], wkvu_d[:, :], "sone")
    w_sone = c("sone")
    VOP(lambda e: e.memset(ONES_R[:, :], 1.0), waits=[("sone", w_sone)])
    VOP(lambda e: e.memset(ONES_C[:, :], 1.0))
    VOP(lambda e: e.tensor_copy(GW[:, :], GW32[:, :]))
    VOP(lambda e: e.tensor_copy(LBB[:, :], LBB32[:, :]))
    VOP(lambda e: e.tensor_copy(WKVD[:, :], W32[:, 0:1024]))
    VOP(lambda e: e.tensor_copy(WKVU[:, :], W32[:, 1024:3072]))
    v_consts = c("vc")
    VOP(lambda e: e.memset(QKM[:, :], 0.0))
    VOP(lambda e: e.memset(
        VB[:, :].rearrange("p (k h) -> p k h", h=65)[:, :, 64:65], 1.0))
    v_vones = c("vc")

    PSBP = [PSB, PSD4[:, 0:512].bitcast(BF16)]
    x_free = {0: 0, 1: 0}
    sq_free = {0: 0, 1: 0}
    xnt_free = {0: 0, 1: 0}
    psb_evacp = {0: 0, 1: 0}
    lg_free_p = {0: 0, 1: 0}
    expm_read = 0
    s_expm = 0
    s_sq = 0
    s_rstd = 0
    v_xnt_prev = 0
    v_c4_prev = 0
    s_latcopy = 0
    pend_lat = None
    xdma = {}

    def emit_xdma(t):
        par = t % 2
        sxn = "sxA" if par == 0 else "sxB"
        xt = XSTG[:, 1024 * par: 1024 * par + 1024]
        wts = [("vc", x_free[par]), ("sc", sq_free[par])]
        DMA("sync", xt[:, 0:512], x_d[128 * t: 128 * t + 128, 0:512], sxn,
            waits=wts)
        DMA("gpsimd", xt[:, 512:1024], x_d[128 * t: 128 * t + 128, 512:1024],
            "gxA" if par == 0 else "gxB", waits=wts)
        xdma[t] = [(sxn, c(sxn)),
                   ("gxA" if par == 0 else "gxB",
                    c("gxA" if par == 0 else "gxB"))]

    def emit_sums(t):
        nonlocal s_sq
        par = t % 2
        xt = XSTG[:, 1024 * par: 1024 * par + 1024]
        SOP(lambda e, x=xt: e.activation(SQ, x, AF.Copy, accum_out=C1[:, :]),
            waits=xdma[t] + [("vc", v_c4_prev)])
        sq_free[par] = SOP(lambda e, x=xt: e.activation(SQ, x, AF.Square,
                                                        accum_out=C2[:, :]))
        s_sq = SPUB()

    def emit_latent_mm(s):
        pd = PSD1 if s < 2 else PSD2
        sl = pd[:, 512 * (s % 2): 512 * (s % 2) + 512]
        for ci in range(HC):
            MM(sl, WKVD[:, 128 * ci: 128 * ci + 128],
               XN[:, T * ci + 512 * s: T * ci + 512 * s + 512],
               start=(ci == 0), stop=(ci == HC - 1), inc=(ci == HC - 1),
               waits=([("sc", psb_evacp[0]), ("sc", psb_evacp[1]),
                       ("vc", v_consts)] if ci == 0 else ()))
        return (s, sl, c("tn"))

    def emit_latent_copy(pend):
        s, sl, t_lat = pend
        return SOP(lambda e, sl=sl, ss=s:
                   e.copy(LATB[:, 512 * ss: 512 * ss + 512], sl),
                   waits=[("tn", t_lat)])

    def emit_routing_a(tt):
        nonlocal expm_read
        expm_read = VOP(lambda e: e.tensor_scalar_mul(SCRS[:, :], EXPM[:, :],
                                                      -1.0),
                        waits=[("sc", s_expm)])
        VOPW(lambda e: e.max(out=T8[:, :], in_=SCRS[:, :]))
        VOPW(lambda e: e.tensor_scalar(RW[:, 0:2], T8[:, 0:2], -1.0, 1.0,
                                       ALU.mult, ALU.add))
        VOPW(lambda e: e.reciprocal(RW[:, 2:3], RW[:, 0:1]))
        VOP(lambda e: e.reciprocal(RW[:, 3:4], RW[:, 1:2]))
        VOPW(lambda e: e.tensor_add(RW[:, 4:5], RW[:, 2:3], RW[:, 3:4]))
        VOPW(lambda e: e.tensor_scalar_add(RW[:, 4:5], RW[:, 4:5], 1e-8))
        VOPW(lambda e: e.reciprocal(RW[:, 5:6], RW[:, 4:5]))
        VOPW(lambda e: e.tensor_tensor(RW[:, 6:8], RW[:, 2:4],
                                       RW[:, 5:6].to_broadcast([128, 2]),
                                       ALU.mult))
        VOPW(lambda e: e.tensor_sub(RW[:, 8:9], RW[:, 6:7], RW[:, 7:8]))

    def emit_routing_b(tt):
        VOP(lambda e, t=tt: e.tensor_tensor(SEL[:, 8 * t: 8 * t + 8], SCRS[:, :],
                                            T8[:, 1:2].to_broadcast([128, 8]),
                                            ALU.is_ge))
        VOP(lambda e: e.tensor_tensor(AUX8[:, :], SCRS[:, :],
                                      T8[:, 0:1].to_broadcast([128, 8]),
                                      ALU.is_ge))
        VOPW(lambda e: e.tensor_tensor(AUX8[:, :], AUX8[:, :],
                                       RW[:, 8:9].to_broadcast([128, 8]),
                                       ALU.mult))
        VOPW(lambda e: e.tensor_tensor(AUX8[:, :], AUX8[:, :],
                                       RW[:, 7:8].to_broadcast([128, 8]),
                                       ALU.add))
        VOPW(lambda e, t=tt: e.tensor_mul(WF[:, 8 * t: 8 * t + 8], AUX8[:, :],
                                          SEL[:, 8 * t: 8 * t + 8]))

    emit_xdma(0)
    emit_sums(0)
    for t in range(NT):
        par = t % 2
        xt = XSTG[:, 1024 * par: 1024 * par + 1024]
        if t + 1 < NT:
            emit_xdma(t + 1)
        VOP(lambda e: e.tensor_scalar_mul(C3[:, :], C1[:, :], 1.0 / H),
            waits=[("sc", s_sq)])
        v_c4_prev = VOP(lambda e: e.tensor_scalar(C4[:, :], C2[:, :], 1.0 / H,
                                                  1e-5, ALU.mult, ALU.add))
        VOP(lambda e: e.tensor_mul(C5[:, :], C3[:, :], C3[:, :]))
        VOPW(lambda e: e.tensor_sub(C4[:, :], C4[:, :], C5[:, :]))
        v_var = VPUB()
        SOP(lambda e: e.activation(C6[:, :], C4[:, :], AF.Ln),
            waits=[("vc", v_var), ("vc", v_xnt_prev)])
        s_rstd = SOP(lambda e: e.activation(C6[:, :], C6[:, :], AF.Exp,
                                            scale=-0.5))
        if t + 1 < NT:
            emit_sums(t + 1)
        if pend_lat is not None:
            s_latcopy = emit_latent_copy(pend_lat)
            pend_lat = None
        if t == 0 and dbg:
            VOP(lambda e: e.tensor_copy(DC0[:, 0:1], C4[:, :]))
            VOP(lambda e: e.tensor_copy(DC0[:, 1:2], C6[:, :]),
                waits=[("sc", s_rstd)])
            VOP(lambda e: e.tensor_copy(DC0[:, 2:3], C3[:, :]))
        xnt = XNT[:, 1024 * par: 1024 * par + 1024]
        v_xnt_prev = VOP(
            lambda e, x=xt, o=xnt: e.tensor_scalar(o, x, C3[:, :], C6[:, :],
                                                   ALU.subtract, ALU.mult),
            waits=[("sc", s_rstd), ("tn", xnt_free[par])])
        x_free[par] = v_xnt_prev
        if t > 0:
            emit_routing_a(t - 1)
            emit_routing_b(t - 1)
        for ci in range(HC):
            TR(PSBP[par][:, 128 * ci: 128 * ci + 128],
               XNT[:, 1024 * par + 128 * ci: 1024 * par + 128 * ci + 128],
               waits=([("vc", v_xnt_prev), ("sc", psb_evacp[par]),
                       ("sone", w_sone), ("sc", s_latcopy)]
                      if ci == 0 else ()))
        xnt_free[par] = c("tn")
        t_trp = c("tn")
        psb_evacp[par] = SOP(
            lambda e, tt=t, p=par: e.copy(
                XN[:, :].rearrange("p (c n) -> p c n", c=HC)[:, :,
                                                            128 * tt: 128 * tt + 128],
                PSBP[p][:, :].rearrange("p (c n) -> p c n", c=HC)),
            waits=[("tn", t_trp)])
        slot = PSD0[:, 8 * par: 8 * par + 8]
        for ci in range(HC):
            MM(slot, XN[:, T * ci + 128 * t: T * ci + 128 * t + 128],
               GW[:, 8 * ci: 8 * ci + 8], start=(ci == 0), stop=False,
               waits=([("sc", psb_evacp[par]), ("sc", lg_free_p[par]),
                       ("vc", v_consts)] if ci == 0 else ()), inc=False)
        MM(slot, ONES_R[:, :], LBB[:, :], start=False, stop=True)
        t_lg = c("tn")
        s_expm = SOP(lambda e, sl=slot: e.activation(EXPM[:, :], sl, AF.Exp,
                                                     scale=-1.0),
                     waits=[("tn", t_lg), ("vc", expm_read)])
        lg_free_p[par] = s_expm
        if t % 4 == 3:
            pend_lat = emit_latent_mm(t // 4)
    emit_routing_a(NT - 1)
    emit_routing_b(NT - 1)
    if pend_lat is not None:
        s_latcopy = emit_latent_copy(pend_lat)
        pend_lat = None

    VOP(lambda e: e.tensor_copy(SELB[:, :], SEL[:, :]))
    VOP(lambda e: e.tensor_copy(WFB[:, :], WF[:, :]))
    v_sb = VPUB()
    t_s1 = MM(PSD4[0:1, 0:128], ONES_C[:, :], SELB[:, :], start=True, stop=True,
              waits=[("vc", v_sb), ("sc", psb_evacp[1])])
    t_s2 = MM(PSD4[0:1, 128:256], ONES_C[:, :], WFB[:, :], start=True, stop=True)
    VOP(lambda e: e.reduce_sum(
        STATS[:, 0:8], PSD4[0:1, 0:128].rearrange("p (t e) -> p e t", e=8),
        axis=AXX), waits=[("tn", t_s2)])
    VOPW(lambda e: e.reduce_sum(
        STATS[:, 8:16], PSD4[0:1, 128:256].rearrange("p (t e) -> p e t", e=8),
        axis=AXX))
    v_stats = VPUB()
    DMA("gpsimd", cci_d[:, :], STATS[:, :], "gcc", waits=[("vc", v_stats)])
    do("gpsimd", lambda e: e.collective_compute(
        "AllReduce", ALU.add, replica_groups=[list(range(8))],
        ins=[cci_d.ap()], outs=[cco_d.ap()]), inc="gc", amt=1,
        waits=[("gcc", c("gcc"))])
    DMA("gpsimd", GSTATS[:, :], cco_d[:, :], "gcc", waits=[("gc", c("gc"))])
    w_gst = c("gcc")
    if dbg:
        DMA("gpsimd", dgst_d[:, :], GSTATS[:, :], "gout", waits=[("gcc", w_gst)])
        DMA("gpsimd", dxn_d[:, :], XN[:, :], "gout",
            waits=[("sc", psb_evacp[0]), ("sc", psb_evacp[1])])
        DMA("gpsimd", dxstg_d[:, :], XSTG[:, :], "gout", waits=[("vc", c("vc"))])
        DMA("gpsimd", dsq_d[:, :], SQ, "gout")
        DMA("gpsimd", dxnt_d[:, :], AOUT[:, 1024:2048].bitcast(BF16), "gout")
        DMA("gpsimd", dscrs_d[:, :], SCRS[:, :], "gout")
        DMA("gpsimd", dsel_d[:, :], SEL[:, :], "gout")
        DMA("gpsimd", dstats_d[:, :], STATS[:, :], "gout", waits=[("vc", v_stats)])
        DMA("gpsimd", dexpm_d[:, :], EXPM[:, :], "gout")
        DMA("gpsimd", dt8_d[:, :], T8[:, :], "gout")
        v_lgc = VOP(lambda e: e.tensor_copy(AUX8[:, :], PSD0[:, 0:8]),
                    waits=[("sc", c("sc"))])
        DMA("gpsimd", dlg_d[:, :], AUX8[:, :], "gout", waits=[("vc", v_lgc)])
        DMA("gpsimd", dc0_d[:, :], DC0[:, :], "gout")

    s_latb = c("sc")
    if dbg:
        DMA("gpsimd", dlat_d[:, :], LATB[:, :], "gout", waits=[("sc", s_latb)])

    s_latb = c("sc")
    ps0_free = [s_latb, s_latb]
    v_pre_k = c("vc")
    for c2 in range(HC):
        for s in range(4):
            par = (c2 * 4 + s) % 2
            sl = PSD0[:, 512 * par: 512 * par + 512]
            MM(sl, WKVU[:, 128 * c2: 128 * c2 + 128],
               LATB[:, 512 * s: 512 * s + 512], start=True, stop=True,
               waits=[("sc", s_latb), ("sc", ps0_free[par]), ("vc", v_pre_k)])
            t_k = c("tn")
            ps0_free[par] = SOP(
                lambda e, sl=sl, cc=c2, ss=s:
                e.copy(KB[:, T * cc + 512 * ss: T * cc + 512 * ss + 512], sl),
                waits=[("tn", t_k)])
            VOP(lambda e, cc=c2, ss=s: e.reduce_max(
                QKM[:, 32 + 4 * cc + ss: 33 + 4 * cc + ss],
                KB[:, T * cc + 512 * ss: T * cc + 512 * ss + 512], axis=AXX,
                apply_absolute_value=True), waits=[("sc", ps0_free[par])])

    v_free = [c("vc"), c("vc")]
    for kc in range(NT):
        par = kc % 2
        pd = PSD1 if par == 0 else PSD2
        for s2 in range(2):
            MM(pd[:, 512 * s2: 512 * s2 + 512],
               LATB[:, 128 * kc: 128 * kc + 128],
               WKVU[:, 1024 + 512 * s2: 1024 + 512 * s2 + 512],
               start=True, stop=True,
               waits=([("sc", s_latb), ("vc", v_free[par]), ("vc", v_vones)]
                      if s2 == 0 else ()))
        t_v = c("tn")
        v_free[par] = VOP(
            lambda e, pd=pd, k=kc: e.tensor_copy(
                VB[:, 1040 * k: 1040 * k + 1040]
                .rearrange("p (h d) -> p h d", d=65)[:, :, 0:64],
                pd[:, :].rearrange("p (h d) -> p h d", d=64)),
            waits=[("tn", t_v)])
    v_vdone = c("vc")
    t_kv_mm = c("tn")

    for qi in (2, 3, 0, 1):
        v_wq = stream_half(
            r3(wq_d[256 * qi: 256 * qi + 256, :]),
            WQB[:, 2048 * qi: 2048 * qi + 2048],
            extra=([("vc", v_consts)] if qi >= 2 else
                   [("vc", v_consts), ("tn", t_kv_mm), ("vc", v_vdone),
                    ("sc", c("sc"))]), ndim3=2)

    ps0q_free = [c("sc"), c("sc")]
    for c2 in range(HC):
        for s in range(4):
            par = (c2 * 4 + s) % 2
            sl = PSD0[:, 512 * par: 512 * par + 512]
            for ci in range(HC):
                MM(sl, WQB[:, 1024 * ci + 128 * c2: 1024 * ci + 128 * c2 + 128],
                   XN[:, T * ci + 512 * s: T * ci + 512 * s + 512],
                   start=(ci == 0), stop=(ci == HC - 1),
                   waits=([("vcast", v_wq), ("sc", ps0q_free[par])]
                          if ci == 0 else ()))
            t_q = c("tn")
            ps0q_free[par] = SOP(
                lambda e, sl=sl, cc=c2, ss=s:
                e.activation(QB[:, T * cc + 512 * ss: T * cc + 512 * ss + 512], sl,
                             AF.Copy, scale=0.125),
                waits=[("tn", t_q)])
            VOP(lambda e, cc=c2, ss=s: e.reduce_max(
                QKM[:, 4 * cc + ss: 4 * cc + ss + 1],
                QB[:, T * cc + 512 * ss: T * cc + 512 * ss + 512], axis=AXX,
                apply_absolute_value=True), waits=[("sc", ps0q_free[par])])
    s_q = c("sc")
    t_qmm = c("tn")
    if dbg:
        DMA("gpsimd", dq_d[:, :], QB[:, :], "gout", waits=[("sc", s_q)])
        DMA("gpsimd", dk_d[:, :], KB[:, :], "gout", waits=[("sc", s_q)])

    for qi in range(4):
        v_wout = stream_half(
            r3(wout_d[256 * qi: 256 * qi + 256, :]),
            WOUTB[:, 2048 * qi: 2048 * qi + 2048],
            extra=[("tn", t_qmm)], ndim3=2)

    VOPW(lambda e: e.reduce_max(QKM[:, 64:65], QKM[:, 0:32], axis=AXX))
    VOPW(lambda e: e.reduce_max(QKM[:, 65:66], QKM[:, 32:64], axis=AXX))
    VOPW(lambda e: e.tensor_copy(QK2[:, :], QKM[:, 64:66]))
    v_qk2 = VPUB()
    t_tr1 = TR(PSB[0:1, 0:128], QK2[:, 0:1],
               waits=[("vc", v_qk2), ("sc", psb_evacp[0])])
    v_mx1 = VOP(lambda e: e.reduce_max(MX2[:, 0:1], PSB[0:1, 0:128], axis=AXX),
                waits=[("tn", t_tr1)])
    t_tr2 = TR(PSB[0:1, 128:256], QK2[:, 1:2], waits=[("vc", v_mx1)])
    VOP(lambda e: e.reduce_max(MX2[:, 1:2], PSB[0:1, 128:256], axis=AXX),
        waits=[("tn", t_tr2)])
    VOPW(lambda e: e.tensor_mul(MR2[:, 0:1], MX2[:, 0:1], MX2[:, 1:2]))
    VOPW(lambda e: e.tensor_scalar(MR2[:, 1:2], MR2[:, 0:1], -64.0, 20.0,
                                   ALU.mult, ALU.add))
    VOPW(lambda e: e.tensor_copy(MR2B[:, :], MR2[:, 1:2]))
    v_negb = VPUB()
    t_negb = MM(PSD4[:, 264:265], ONES_R[:, :], MR2B[:, :], start=True, stop=True,
                waits=[("vc", v_negb), ("vc", v_stats)])
    VOP(lambda e: e.tensor_copy(NEGB[:, :], PSD4[:, 264:265]),
        waits=[("tn", t_negb)])
    v_negb2 = VPUB()
    if dbg:
        DMA("gpsimd", dnegb_d[:, :], NEGB[:, :], "gout", waits=[("vc", v_negb2)])

    barrier()

    PTE = [PT[0], PT[1], AOUT[:, 0:512].bitcast(BF16)]
    PTO = [bfv(OFF_R1, 2048), XSTG[:, 1536:2048].bitcast(BF16),
           AOUT[:, 512:1024].bitcast(BF16)]
    SROW_E = SROW
    SROW_O = AOUT[0:1, 1024:1536].bitcast(BF16)
    sc_free_e = c("sc")
    sc_free_o = c("sc")
    pt_e_free = {0: 0, 1: 0, 2: 0}
    pt_o_free = {0: 0, 1: 0, 2: 0}
    ype_free = c("vc")
    ypo_free = c("vc")
    yq_free = 0
    aout_free = [c("gout"), c("gout")]
    yodd_free = 0
    first_exp = True
    pend_tails = None
    qg_guard = 0
    srow_free_e = 0
    srow_free_o = 0
    t_ave = t_avo = 0

    def emit_tail_a(h, yp, t_av):
        c2h = h // 2
        srw = SROW_E if h % 2 == 0 else SROW_O
        sfree = srow_free_e if h % 2 == 0 else srow_free_o
        SOP(lambda e, yp=yp: e.activation(SROW32, yp[64:65, 0:1024], AF.Ln),
            waits=[("tn", t_av)])
        SOP(lambda e, srw=srw: e.activation(srw, SROW32, AF.Exp, scale=-1.0),
            waits=[("tn", sfree)])
        s_srow = SPUB()
        if h % 2 == 0:
            ydst = YQ[0:64, 1024 * c2h: 1024 * c2h + 1024]
            w_ev = [("tn", t_av), ("tn", yq_free)]
        else:
            ydst = YODD[0:64, :]
            w_ev = [("tn", t_av), ("gyo", yodd_free)]
        VOP(lambda e, yp=yp, yd=ydst: e.tensor_copy(yd, yp[0:64, 0:1024]),
            waits=w_ev)
        v_yev = VPUB()
        return (h, yp, ydst, srw, s_srow, v_yev)

    def emit_tail_b(tb):
        nonlocal yodd_free, srow_free_e, srow_free_o, ype_free, ypo_free
        (h, yp, ydst, srw, s_srow, v_yev) = tb
        for s2 in range(2):
            MM(yp[0:64, 512 * s2: 512 * s2 + 512], ONES_R[:, 0:64],
               srw[:, 512 * s2: 512 * s2 + 512], start=True, stop=True,
               waits=([("sc", s_srow), ("vc", v_yev)] if s2 == 0 else ()))
        t_bc = c("tn")
        if h % 2 == 0:
            srow_free_e = t_bc
        else:
            srow_free_o = t_bc
        VOP(lambda e, yd=ydst, yp=yp: e.tensor_mul(yd, yd, yp[0:64, 0:1024]),
            waits=[("tn", t_bc)])
        v_div = VPUB()
        if h % 2 == 0:
            ype_free = v_div
        else:
            ypo_free = v_div
            yodd_free = DMA("sync",
                            YQ[64:128, 1024 * (h // 2): 1024 * (h // 2) + 1024],
                            YODD[0:64, :], "gyo",
                            waits=[("vc", v_div), ("tn", yq_free)])

    for qg in range(NQG):
        q0g = QGS * qg
        kmax = 8 * (qg + 1)
        for pp in range(NH // 2):
            he, ho = 2 * pp, 2 * pp + 1

            def qkt2(kc, xg_o=()):
                qlo = max(q0g, 128 * kc)
                MM(PSD0[:, 0:512], IDENT[:, :], QB[:, 0:512],
                   start=True, stop=True, inc=False,
                   waits=[("sc", sc_free_e)])
                t_q = {}
                for (rr_, pair, scf, xg2) in ((0, PSD0, sc_free_e, ()),
                                              (64, PSD4, sc_free_o, xg_o)):
                    b0 = qlo
                    first_mm = True
                    while b0 < q0g + QGS:
                        b1 = min(q0g + QGS, (b0 // 512 + 1) * 512)
                        MM(pair[:, b0 - q0g: b1 - q0g],
                           KB[rr_: rr_ + 64,
                              T * pp + 128 * kc: T * pp + 128 * kc + 128],
                           QB[rr_: rr_ + 64, T * pp + b0: T * pp + b1],
                           start=True, stop=True, inc=(b1 == q0g + QGS),
                           waits=(([("sc", scf)] + list(xg2))
                                  if first_mm else ()))
                        first_mm = False
                        b0 = b1
                    t_q[rr_] = c("tn")
                return (kc, qlo, t_q[0], t_q[64])

            def exp2(st):
                nonlocal first_exp, sc_free_e, sc_free_o
                (kc, qlo, t_qe, t_qo) = st
                a0 = qlo - q0g
                w_e = [("tn", t_qe), ("tn", pt_e_free[kc % 3])]
                if first_exp:
                    w_e.append(("vc", v_negb2))
                    first_exp = False
                sc_free_e = SOP(
                    lambda e, k=kc, a=a0: e.activation(
                        PTE[k % 3][:, a: 1024], PSD0[:, a: 1024], AF.Exp,
                        bias=NEGB[:, :]), waits=w_e)
                sc_free_o = SOP(
                    lambda e, k=kc, a=a0: e.activation(
                        PTO[k % 3][:, a: 1024], PSD4[:, a: 1024], AF.Exp,
                        bias=NEGB[:, :]),
                    waits=[("tn", t_qo), ("tn", pt_o_free[kc % 3])])
                s_e, s_o = sc_free_e, sc_free_o
                w_mask = []
                if kc >= 8 * qg:
                    VOP(lambda e, k=kc, a=a0: e.tensor_mul(
                        PTE[k % 3][:, a: a + 128], PTE[k % 3][:, a: a + 128],
                        TRI[:, :]), waits=[("sc", s_e)])
                    VOP(lambda e, k=kc, a=a0: e.tensor_mul(
                        PTO[k % 3][:, a: a + 128], PTO[k % 3][:, a: a + 128],
                        TRI[:, :]), waits=[("sc", s_o)])
                    w_mask = [("vc", VPUB())]
                return (kc, qlo, s_e, s_o, w_mask)

            def av2(st2):
                nonlocal t_ave, t_avo
                (kc, qlo, s_e, s_o, w_mask) = st2
                for (hh, ypx, PTx, s_x, pfx, yfx) in (
                        (he, PSD1, PTE, s_e, pt_e_free, ype_free),
                        (ho, PSD2, PTO, s_o, pt_o_free, ypo_free)):
                    b0 = qlo
                    first_av = True
                    while b0 < q0g + QGS:
                        b1 = min(q0g + QGS, (b0 // 512 + 1) * 512)
                        w_av = []
                        if first_av:
                            w_av += [("sc", s_x)] + w_mask
                            if kc == 0:
                                w_av.append(("vc", yfx))
                        bank = (b0 - q0g) // 512
                        pfx[kc % 3] = MM(
                            ypx[0:65, b0 - q0g: b1 - q0g],
                            VB[:, 1040 * kc + 65 * hh:
                               1040 * kc + 65 * hh + 65],
                            PTx[kc % 3][:, b0 - q0g: b1 - q0g],
                            start=(kc == 0),
                            stop=(kc == 8 * qg + 4 * bank + 3),
                            waits=w_av, inc=(b1 == q0g + QGS))
                        first_av = False
                        b0 = b1
                    if hh == he:
                        t_ave = c("tn")
                    else:
                        t_avo = c("tn")

            xg_o = [("vc", qg_guard)] if pp == 0 else []
            sts = {0: qkt2(0, xg_o)}
            exps = {}
            ta_e = ta_o = None
            for kc in range(1, kmax):
                exps[kc - 1] = exp2(sts.pop(kc - 1))
                sts[kc] = qkt2(kc)
                if kc == 1 and pend_tails is not None:
                    ta_e = emit_tail_a(*pend_tails[0])
                if kc == 2 and pend_tails is not None:
                    ta_o = emit_tail_a(*pend_tails[1])
                    emit_tail_b(ta_e)
                    ta_e = None
                if kc == 3 and ta_o is not None:
                    emit_tail_b(ta_o)
                    ta_o = None
                    pend_tails = None
                if kc - 3 in exps:
                    av2(exps.pop(kc - 3))
            exps[kmax - 1] = exp2(sts.pop(kmax - 1))
            for k in sorted(exps):
                av2(exps.pop(k))
            pend_tails = [(he, PSD1, t_ave), (ho, PSD2, t_avo)]
        t_av_all = c("tn")
        if pend_tails is not None:
            tfa = emit_tail_a(*pend_tails[0])
            tfb = emit_tail_a(*pend_tails[1])
            emit_tail_b(tfa)
            emit_tail_b(tfb)
            pend_tails = None
        w_op = [("gyo", c("gyo")), ("vc", c("vc")), ("vcast", v_wout)]
        po_free = 0
        for tcl in range(8):
            tglob = 8 * qg + tcl
            for s2 in range(2):
                for c2 in range(HC):
                    MM(PSD4[:, 0:512],
                       YQ[:, 1024 * c2 + 128 * tcl: 1024 * c2 + 128 * tcl + 128],
                       WOUTB[:, 1024 * c2 + 512 * s2: 1024 * c2 + 512 * s2 + 512],
                       start=(c2 == 0), stop=(c2 == HC - 1), inc=(c2 == HC - 1),
                       waits=(w_op + [("vc", po_free)]) if c2 == 0 else ())
                    w_op = []
                t_po = c("tn")
                po_free = VOP(
                    lambda e, par=tcl % 2, s2=s2:
                    e.tensor_copy(AOUT[:, 1024 * par + 512 * s2:
                                       1024 * par + 512 * s2 + 512], PSD4[:, 0:512]),
                    waits=[("tn", t_po)] +
                          ([("gout", aout_free[tcl % 2])] if s2 == 0 else []))
            v_po = VPUB()
            aout_free[tcl % 2] = DMA(
                "gpsimd", attn_d[128 * tglob: 128 * tglob + 128, :],
                AOUT[:, 1024 * (tcl % 2): 1024 * (tcl % 2) + 1024], "gout",
                waits=[("vc", v_po)])
        yq_free = c("tn")
        qg_guard = c("vc")
    g_attn = c("gout")

    VOP(lambda e: e.tensor_scalar_max(GSTATS[:, 0:8], GSTATS[:, 0:8], 1.0),
        waits=[("gcc", w_gst)])
    v_cnt = VPUB()
    SOP(lambda e: e.activation(GLN[:, :], GSTATS[:, 0:8], AF.Ln),
        waits=[("vc", v_cnt)])
    SOP(lambda e: e.activation(GLN[:, :], GLN[:, :], AF.Exp, scale=-1.0))
    s_gln = SPUB()
    VOP(lambda e: e.tensor_mul(WMB[:, :], GSTATS[:, 8:16], GLN[:, :]),
        waits=[("sc", s_gln)])
    v_wmb = VPUB()

    nslot[0] = 4
    stg_gate[2] = [("tn", yq_free), ("gout", g_attn)]
    stg_gate[3] = [("tn", srow_free_e), ("sc", c("sc"))]
    wpar[0] += (-wpar[0]) % 4
    pre_snap = [(s, v) for s, v in cnt.items() if v > 0]
    ewr_g = {}
    ewr_u = {}
    ewr_d = {}
    ew_users = {0: 0, 1: 0}

    def stream_mat(dr, e_, dst, extra=()):
        rows = dr.shape[1]
        for hf in range(2):
            hr = rows // 2
            src = r3(dr[e_, hr * hf: hr * hf + hr, :])
            r = stream_half(src, dst[:, 2048 * hf: 2048 * hf + 2048],
                            extra, ndim3=hr // 128)
        return r

    ewr_g[0] = stream_mat(egwT_d, 0, EW[0][0], [("tn", t_av_all)])
    ewr_u[0] = stream_mat(eupT_d, 0, EW[0][1])
    ewr_d[0] = stream_mat(edwT_d, 0, EW[0][2])
    ewr_g[1] = stream_mat(egwT_d, 1, EW[1][0], [("tn", yq_free)])
    ewr_u[1] = stream_mat(eupT_d, 1, EW[1][1])
    ewr_d[1] = stream_mat(edwT_d, 1, EW[1][2])

    barrier(pre_snap)

    v_wmean = 0
    sg_ready = {}
    su_ready = {}
    sgc_users = {0: 0, 1: 0}

    def col_chunk(dr, si):
        return r3(dr[:, 128 * si: 128 * si + 128])

    def stream_gu(si, extra=()):
        p = si % 2
        sg_ready[si] = stream_half(col_chunk(sgwT_d, si), SGC[p][:, 0:1024],
                                   list(extra) + [("tn", sgc_users[p])],
                                   ncols=1024, ndim3=8)
        su_ready[si] = stream_half(col_chunk(supT_d, si), SUC[p][:, 0:1024],
                                   list(extra) + [("tn", sgc_users[p])],
                                   ncols=1024, ndim3=8)

    gu_free = {"g0": 0, "g1": 0}
    gu_u = [0, 0]
    down_free = [0, 0]
    acc_done = [0] * NT

    for e_ in range(E):
        p = e_ % 2
        GTb, UTb, DTb = EW[p]
        w_w = [("vcast", ewr_g[e_])]
        g_users = 0
        for ic in range(4):
            for hh in range(HC):
                for s in range(4):
                    pd = PSD1 if s < 2 else PSD2
                    sl = pd[:, 512 * (s % 2): 512 * (s % 2) + 512]
                    MM(sl, GTb[:, 512 * hh + 128 * ic: 512 * hh + 128 * ic + 128],
                       XN[:, T * hh + 512 * s: T * hh + 512 * s + 512],
                       start=(hh == 0), stop=(hh == HC - 1),
                       inc=(hh == HC - 1 and s == 3),
                       waits=(w_w + [("sc", gu_free["g0"]), ("sc", gu_free["g1"])])
                       if (hh == 0 and s == 0) else ())
                    w_w = []
            t_g = c("tn")
            if ic == 3:
                g_users = t_g
            for s in range(4):
                pd = PSD1 if s < 2 else PSD2
                sl = pd[:, 512 * (s % 2): 512 * (s % 2) + 512]
                k = SOP(lambda e, sl=sl, s=s: e.activation(
                    SGE[:, 512 * s: 512 * s + 512], sl, AF.Silu),
                    waits=([("tn", t_g), ("vc", gu_u[0]), ("vc", gu_u[1])]
                           if s == 0 else [("tn", t_g)]))
                if s == 1:
                    gu_free["g0"] = k
                if s == 3:
                    gu_free["g1"] = k
            s_silu = c("sc")
            for hp2 in range(2):
                updn = PSD0 if hp2 == 0 else PSD4
                for hh in range(HC):
                    for s in (2 * hp2, 2 * hp2 + 1):
                        sl = updn[:, 512 * (s % 2): 512 * (s % 2) + 512]
                        MM(sl, UTb[:, 512 * hh + 128 * ic: 512 * hh + 128 * ic + 128],
                           XN[:, T * hh + 512 * s: T * hh + 512 * s + 512],
                           start=(hh == 0), stop=(hh == HC - 1),
                           inc=(hh == HC - 1 and s == 2 * hp2 + 1),
                           waits=([("vc", gu_u[hp2]), ("vc", down_free[hp2])] +
                                  ([("vcast", ewr_u[e_])]
                                   if (ic == 0 and hp2 == 0) else [])
                                  if (hh == 0 and s == 2 * hp2) else ()))
                t_u = c("tn")
                for s in (2 * hp2, 2 * hp2 + 1):
                    sl = updn[:, 512 * (s % 2): 512 * (s % 2) + 512]
                    gu_u[hp2] = VOP(
                        lambda e, sl=sl, s=s, ic=ic: e.tensor_mul(
                            H_CH[ic][:, 512 * s: 512 * s + 512],
                            SGE[:, 512 * s: 512 * s + 512], sl),
                        waits=[("tn", t_u), ("sc", s_silu)])
        u_users = c("tn")
        if e_ == 0:
            t_wmb = MM(PSD4[:, 256:264], ONES_R[:, :], WMB[:, :], start=True,
                       stop=True, waits=[("vc", v_wmb), ("vc", gu_u[1])])
            VOP(lambda e: e.tensor_copy(WMEANB[:, :], PSD4[:, 256:264]),
                waits=[("tn", t_wmb)])
            v_wmean = VPUB()
            down_free[1] = v_wmean
        v_h = c("vc")
        for tc in range(NT):
            pd = PSD0 if tc % 2 == 0 else PSD4
            for ic in range(4):
                for s2 in range(2):
                    MM(pd[:, 512 * s2: 512 * s2 + 512],
                       H_CH[ic][:, 128 * tc: 128 * tc + 128],
                       DTb[:, 1024 * ic + 512 * s2: 1024 * ic + 512 * s2 + 512],
                       start=(ic == 0), stop=(ic == 3),
                       inc=(ic == 3 and s2 == 1),
                       waits=([("vc", v_h), ("vc", down_free[tc % 2]),
                               ("vc", gu_u[tc % 2])] +
                              ([("vcast", ewr_d[e_])] if tc == 0 else [])
                              if (s2 == 0 and ic == 0) else ()))
            t_dn = c("tn")
            if tc == 6 and e_ + 2 < E:
                ewr_g[e_ + 2] = stream_mat(egwT_d, e_ + 2, GTb,
                                           [("tn", g_users)])
                ewr_u[e_ + 2] = stream_mat(eupT_d, e_ + 2, UTb,
                                           [("tn", u_users)])
            v_ws = VOP(lambda e, tc=tc, ee=e_: e.tensor_mul(
                WS[:, :], SEL[:, 8 * tc + ee: 8 * tc + ee + 1],
                WMEANB[:, ee: ee + 1]),
                waits=([("vc", v_wmean)] if (e_ == 0 and tc == 0) else ()))
            if e_ == 0:
                down_free[tc % 2] = VOP(
                    lambda e, pd=pd, tc=tc: e.tensor_scalar(
                        ACC[:, 1024 * tc: 1024 * tc + 1024], pd[:, 0:1024],
                        WS[:, :], None, ALU.mult),
                    waits=[("tn", t_dn), ("vc", v_ws)] +
                          ([("tn", yq_free), ("gout", g_attn)] if tc == 0 else []))
            else:
                down_free[tc % 2] = VOP(
                    lambda e, pd=pd, tc=tc: e.scalar_tensor_tensor(
                        ACC[:, 1024 * tc: 1024 * tc + 1024], pd[:, 0:1024], WS[:, :],
                        ACC[:, 1024 * tc: 1024 * tc + 1024], ALU.mult, ALU.add),
                    waits=[("tn", t_dn), ("vc", v_ws)])
            acc_done[tc] = down_free[tc % 2]
        ew_users[p] = c("tn")
        if e_ + 2 < E:
            ewr_d[e_ + 2] = stream_mat(edwT_d, e_ + 2, DTb,
                                       [("tn", ew_users[p])])
        if e_ == E - 1:
            t_e7dn = [("tn", c("tn"))]
            stream_gu(0, t_e7dn)
            stream_gu(1, t_e7dn)

    gu_s = {"g0": c("sc"), "g1": 0}
    gu_su = [c("vc"), c("vc")]
    sd_ready = 0
    sdown_free = [c("vc"), c("vc")]
    hs_done = 0
    for si in range(8):
        p = si % 2
        for hh in range(HC):
            for s in range(4):
                pd = PSD1 if s < 2 else PSD2
                sl = pd[:, 512 * (s % 2): 512 * (s % 2) + 512]
                MM(sl, SGC[p][:, 128 * hh: 128 * hh + 128],
                   XN[:, T * hh + 512 * s: T * hh + 512 * s + 512],
                   start=(hh == 0), stop=(hh == HC - 1),
                   inc=(hh == HC - 1 and s == 3),
                   waits=([("vcast", sg_ready[si]),
                           ("sc", gu_s["g0"]), ("sc", gu_s["g1"]),
                           ("vc", sdown_free[0]), ("vc", sdown_free[1])]
                          if (hh == 0 and s == 0) else ()))
        t_g = c("tn")
        for s in range(4):
            pd = PSD1 if s < 2 else PSD2
            sl = pd[:, 512 * (s % 2): 512 * (s % 2) + 512]
            k = SOP(lambda e, sl=sl, s=s: e.activation(
                SGS[:, 512 * s: 512 * s + 512], sl, AF.Silu),
                waits=([("tn", t_g), ("vc", gu_su[0]), ("vc", gu_su[1])]
                       if s == 0 else [("tn", t_g)]))
            if s == 1:
                gu_s["g0"] = k
            if s == 3:
                gu_s["g1"] = k
        s_silu = c("sc")
        for hp2 in range(2):
            updn = PSD0 if hp2 == 0 else PSD4
            for hh in range(HC):
                for s in (2 * hp2, 2 * hp2 + 1):
                    sl = updn[:, 512 * (s % 2): 512 * (s % 2) + 512]
                    MM(sl, SUC[p][:, 128 * hh: 128 * hh + 128],
                       XN[:, T * hh + 512 * s: T * hh + 512 * s + 512],
                       start=(hh == 0), stop=(hh == HC - 1),
                       inc=(hh == HC - 1 and s == 2 * hp2 + 1),
                       waits=([("vcast", su_ready[si]), ("vc", gu_su[hp2])]
                              if (hh == 0 and s == 2 * hp2) else ()))
            t_u = c("tn")
            for s in (2 * hp2, 2 * hp2 + 1):
                sl = updn[:, 512 * (s % 2): 512 * (s % 2) + 512]
                gu_su[hp2] = VOP(
                    lambda e, sl=sl, s=s, si=si: e.tensor_mul(
                        HS_CH[si][:, 512 * s: 512 * s + 512],
                        SGS[:, 512 * s: 512 * s + 512], sl),
                    waits=[("tn", t_u), ("sc", s_silu)])
        sgc_users[p] = c("tn")
        if si + 2 < 8:
            stream_gu(si + 2)
        sd_ready = stream_half(
            sdwT_d[128 * si: 128 * si + 128, :],
            SD[:, 1024 * si: 1024 * si + 1024],
            extra=[("tn", ew_users[0])], ncols=1024)
    v_hs = c("vc")

    shared_done = [0] * NT
    at_free = [c("vc"), c("vc")]

    def emit_p5(tc):
        par = tc % 2
        sxn = "sxA" if par == 0 else "sxB"
        at = ACC[:, 1024 * tc: 1024 * tc + 1024]
        stg = AOUT[:, 1024 * par: 1024 * par + 1024]
        xstg = XSTG[:, 1024 * par: 1024 * par + 1024]
        DMA("sync", stg, attn_d[128 * tc: 128 * tc + 128, :], sxn,
            waits=[("gout", g_attn), ("vc", at_free[par]),
                   ("vcast", w32_free[2]), ("vcast", w32_free[3])])
        DMA("sync", xstg, x_d[128 * tc: 128 * tc + 128, :], sxn,
            waits=[("vcast", w32_free[3])])
        w_at = c(sxn)
        VOP(lambda e, at=at, stg=stg: e.tensor_add(at, at, stg),
            waits=[(sxn, w_at), ("vc", shared_done[tc])])
        VOP(lambda e, at=at, xs=xstg: e.tensor_add(at, at, xs))
        at_free[par] = v_fin = VPUB()
        DMA("gpsimd", out_d[128 * tc: 128 * tc + 128, :], at, "sout",
            waits=[("vc", v_fin)])

    for tc in range(NT):
        pd = PSD1 if tc % 2 == 0 else PSD2
        for si in range(8):
            for s2 in range(2):
                MM(pd[:, 512 * s2: 512 * s2 + 512],
                   HS_CH[si][:, 128 * tc: 128 * tc + 128],
                   SD[:, 1024 * si + 512 * s2: 1024 * si + 512 * s2 + 512],
                   start=(si == 0), stop=(si == 7),
                   inc=(si == 7 and s2 == 1),
                   waits=([("vc", v_hs), ("vc", sdown_free[tc % 2]),
                           ("vcast", sd_ready)]
                          if (s2 == 0 and si == 0) else ()))
        t_dn = c("tn")
        VOP(lambda e, pd=pd, tc=tc: e.tensor_tensor(
                ACC[:, 1024 * tc: 1024 * tc + 1024],
                ACC[:, 1024 * tc: 1024 * tc + 1024], pd[:, :], ALU.add),
            waits=[("tn", t_dn), ("vc", acc_done[tc])])
        sdown_free[tc % 2] = VPUB()
        shared_done[tc] = sdown_free[tc % 2]
        emit_p5(tc)
    if dbg:
        DMA("gpsimd", dacc_d[:, :], ACC[:, :], "gout",
            waits=[("vc", max(shared_done))])

    sched["gpsimd"].append(([("sout", c("sout"))], None, None, 0))

    with es:
        with nc.Block() as block:
            def runner(name):
                def _run(eng):
                    for waits, fn, inc, amt in sched[name]:
                        for (s, v) in waits:
                            eng.wait_ge(sems[s], v)
                        if fn is not None:
                            ins = fn(eng)
                            if inc is not None:
                                ins.then_inc(sems[inc], amt)
                return _run
            block.sync(runner("sync"))
            block.gpsimd(runner("gpsimd"))
            block.tensor(runner("tensor"))
            block.vector(runner("vector"))
            block.scalar(runner("scalar"))
    return nc


def prep_inputs(inputs):
    f = np.float32
    g = {}
    g["wq"] = np.ascontiguousarray(np.asarray(inputs["L0_attn_w_q"], f))
    g["wout"] = np.ascontiguousarray(np.asarray(inputs["L0_attn_w_out"], f))
    g["wkvd"] = np.ascontiguousarray(np.asarray(inputs["L0_attn_w_kv_down"], f))
    g["wkvu"] = np.ascontiguousarray(np.asarray(inputs["L0_attn_w_kv_up"], f))
    g["gwT"] = np.ascontiguousarray(np.asarray(inputs["L0_ffn_gate_w"], f).T)
    g["lb"] = np.ascontiguousarray(np.asarray(inputs["L0_ffn_lb_bias"], f)
                                   .reshape(1, 8))
    g["egwT"] = np.ascontiguousarray(
        np.asarray(inputs["L0_ffn_expert_gate_w"], f).transpose(0, 2, 1))
    g["eupT"] = np.ascontiguousarray(
        np.asarray(inputs["L0_ffn_expert_up_w"], f).transpose(0, 2, 1))
    g["edwT"] = np.ascontiguousarray(
        np.asarray(inputs["L0_ffn_expert_down_w"], f).transpose(0, 2, 1))
    g["sgwT"] = np.ascontiguousarray(np.asarray(inputs["L0_ffn_shared_gate_w"], f).T)
    g["supT"] = np.ascontiguousarray(np.asarray(inputs["L0_ffn_shared_up_w"], f).T)
    g["sdwT"] = np.ascontiguousarray(np.asarray(inputs["L0_ffn_shared_down_w"], f).T)
    g["ident"] = np.eye(128, dtype=BFNP)
    r = np.arange(128)
    g["tri"] = (r[:, None] <= r[None, :]).astype(BFNP)
    return g


def kernel(**inputs):
    x = np.ascontiguousarray(np.asarray(inputs["L0_attn_x"], np.float32))
    g = prep_inputs(inputs)
    nc = build_program()
    in_maps = [dict(g, x=np.ascontiguousarray(x[b])) for b in range(8)]
    res = run_bass_kernel_spmd(nc, in_maps, core_ids=list(range(8)))
    return np.stack([r["out"] for r in res.results], axis=0)
